# version 19
# baseline (speedup 1.0000x reference)
import numpy as np
import concourse.bass as bass
import concourse.mybir as mybir
from concourse.bass_utils import run_bass_kernel_spmd

F32 = mybir.dt.float32
BF16 = mybir.dt.bfloat16
I32 = mybir.dt.int32
U8 = mybir.dt.uint8
AF = mybir.ActivationFunctionType
ALU = mybir.AluOpType
AX = mybir.AxisListType.X

N_CORES = 8


def build_nc(NT, KC, HQ, n_cores=N_CORES, use_cc=True, debug=False):
    D = 128 * KC
    TC = 128 * NT
    HL = 1 if TC <= 512 else 2
    NH = TC // HL
    assert NH <= 512
    OC = 3 * HQ
    FS = HQ
    SPLIT = (KC // 2) % 8 == 0
    KH = KC // 2 if SPLIT else KC
    TWQL = "twqB" if SPLIT else "twqA"
    MML = "mmB" if SPLIT else "mmA"

    OVL = (HL == 2 and NT == 8)
    EARLY = (12 if HQ >= 16 else min(4, HQ)) if OVL else 0
    NSTG = max(0, (EARLY - 4) // 2)
    NE3 = min(3, EARLY)

    assert FS >= 2
    eps = 1e-5

    nc = bass.Bass("TRN2", target_bir_lowering=False)

    x_h = nc.dram_tensor("x", [TC, D], F32, kind="ExternalInput").ap()
    W_full = nc.dram_tensor("W", [D, 3 * D], F32, kind="ExternalInput").ap()
    wt_h = nc.dram_tensor("wt", [D, D], F32, kind="ExternalInput").ap()
    bcol_h = nc.dram_tensor("bcol", [128, OC], F32, kind="ExternalInput").ap()
    biascol_h = nc.dram_tensor("biascol", [128, FS], F32, kind="ExternalInput").ap()
    identb_h = nc.dram_tensor("identb", [128, 128], BF16, kind="ExternalInput").ap()
    identf_h = nc.dram_tensor("identf", [128, 128], F32, kind="ExternalInput").ap()
    out_h = nc.dram_tensor("out", [TC, D], F32, kind="ExternalOutput").ap()
    dbg_h = {}
    if debug:
        for nm, rows, nel, dt in [("xnT", 128, KC * TC, BF16), ("xqT", 128, KC * TC, BF16),
                                  ("qT", 128, HQ * TC, BF16), ("Sloc", 128, 3 * HQ, F32),
                                  ("Sglob", 128, 3 * HQ, F32), ("g", 128, HQ, F32),
                                  ("xsrow", 128, TC, F32), ("wscol", 128, HQ, F32),
                                  ("stats", 128, 64, F32)]:
            dbg_h[nm] = nc.dram_tensor("dbg_" + nm, [rows, nel], dt, kind="ExternalOutput").ap()
    cc_in = nc.dram_tensor("cc_in", [128, 3 * HQ], F32)
    cc_out = nc.dram_tensor("cc_out", [128, 3 * HQ], F32)

    xs_dram = nc.dram_tensor("xs_dram", [NT, 128], F32)

    off = {}
    cur = 0

    def place(name, nbytes, align=32):
        nonlocal cur
        cur = (cur + align - 1) // align * align
        off[name] = cur
        cur += nbytes

    place("xstage", 2 * D * 4)
    place("xre", 2 * NT * 128 * 4)
    place("xnT", KC * TC * 2)
    place("xqT", KC * TC * 2)
    place("qT", HQ * TC * 2)
    place("scr0", 2 * D * 4)
    place("scr1", max(D, 2 * TC) * 4)
    place("junk", TC * 4)
    place("junk2", max(D, TC) * 4)
    place("xnbf", D * 2)
    place("xqbf", D * 2)
    place("wbuf", 2 * KC * 256 * 2)
    place("wrow", 2 * D * 4)
    place("xsrow", TC * 4)
    place("stats", 64 * 4)
    place("Sloc", 3 * HQ * 4)
    place("Sglob", 3 * HQ * 4)
    place("gtmp", HQ * 4)
    place("g", HQ * 4)
    place("wscol", HQ * 4)
    place("winv", HQ * 4)
    place("battn", OC * 4)
    place("biasc", FS * 4)
    place("identb", 128 * 2)
    place("identf", 128 * 4)
    place("ones", 128 * 4)
    if 2 * KC * 128 * 2 + 2 * TC * 4 + 2 * TC * 4 <= KC * TC * 2:
        off["wqTb"] = off["xnT"]
        off["outb"] = off["wqTb"] + 2 * KC * 128 * 2
        off["scrO"] = off["outb"] + 2 * TC * 4
    else:
        place("wqTb", 2 * KC * 128 * 2)
        place("outb", 2 * TC * 4)
        place("scrO", 2 * TC * 4)
    if 4 * TC * 4 <= 2 * KC * 256 * 2:
        off["scrU"] = off["wbuf"]
        off["scrU2"] = off["wbuf"] + 2 * TC * 4
    else:
        place("scrU", 2 * TC * 4)
        place("scrU2", 2 * TC * 4)
    off["scrU2hi"] = off["scr1"]
    off["xsrow1"] = off["xsrow"]
    arena_bytes = (cur + 31) // 32 * 32
    assert arena_bytes <= 206 * 1024, f"sbuf over budget: {arena_bytes}"

    arena = nc.alloc_sbuf_tensor("arena", [128, arena_bytes], U8).ap()
    ESZ = {F32: 4, BF16: 2, I32: 4}

    def view(name, dt, nelem, rows=128, byte_off=0):
        o = off[name] + byte_off
        return arena[0:rows, o:o + nelem * ESZ[dt]].bitcast(dt)

    xstage = [view("xstage", F32, D, byte_off=0),
              view("xstage", F32, D, byte_off=D * 4),
              view("wrow", F32, D, byte_off=0)]
    xre = [view("xre", F32, NT * 128, byte_off=i * NT * 128 * 4) for i in range(2)]
    xnT = view("xnT", BF16, KC * TC)
    xqT = view("xqT", BF16, KC * TC)
    qT = view("qT", BF16, HQ * TC)
    scr0 = [view("scr0", F32, D, byte_off=i * D * 4) for i in range(2)]
    scr0_i32 = view("scr0", I32, D)
    scr1 = view("scr1", F32, max(D, 2 * TC))
    junk = view("junk", F32, TC)
    junk2 = view("junk2", F32, max(D, TC))
    scr1_i32 = view("scr1", I32, D)
    xnbf = view("xnbf", BF16, D)
    xqbf = view("xqbf", BF16, D)
    wqbf2 = [xqbf, xnbf]
    wbuf = [view("wbuf", BF16, KC * 256, byte_off=i * KC * 256 * 2) for i in range(2)]
    wrow = [view("wrow", F32, D, byte_off=i * D * 4) for i in range(2)]
    wqTb = [view("wqTb", BF16, KC * 128, byte_off=i * KC * 128 * 2) for i in range(2)]
    if NSTG:
        assert NSTG == 4 and (12 + 4) * TC * 2 <= HQ * TC * 2
        assert KC * 256 * 2 <= min(D * 4, 2 * NT * 128 * 4)
        stg = [view("qT", BF16, KC * 256, byte_off=12 * TC * 2),
               view("wrow", BF16, KC * 256, byte_off=D * 4),
               view("xstage", BF16, KC * 256, byte_off=0),
               view("xre", BF16, KC * 256, byte_off=0)]
    else:
        stg = None
    scrU = [view("scrU", F32, TC, byte_off=i * TC * 4) for i in range(2)]
    scrU2 = [view("scrU2", F32, TC, byte_off=i * TC * 4) for i in range(2)] + \
            [view("scrU2hi", F32, TC, byte_off=i * TC * 4) for i in range(2)]
    scrO = [view("scrO", F32, TC, byte_off=i * TC * 4) for i in range(2)]
    outb = [view("outb", F32, TC, byte_off=i * TC * 4) for i in range(2)]
    xsrow = view("xsrow", F32, TC)
    xsrow1 = view("xsrow1", F32, TC, rows=1)
    stats = view("stats", F32, 64)
    Sloc = view("Sloc", F32, 3 * HQ)
    Sglob = view("Sglob", F32, 3 * HQ)
    gtmp = view("gtmp", F32, HQ)
    g_sb = view("g", F32, HQ)
    wscol = view("wscol", F32, HQ)
    winv = view("winv", F32, HQ)
    battn = view("battn", F32, OC)
    biasc = view("biasc", F32, FS)
    identb = view("identb", BF16, 128)
    identf = view("identf", F32, 128)
    ones_sb = view("ones", F32, 128, rows=1)

    C_SUM, C_SSQ, C_M, C_M2, C_A, C_R, C_NMR, C_XMAX = range(8)

    def stc(c, i=0):
        o = 2 * c + (i & 1)
        return stats[:, o:o + 1]

    onec = stats[:, 56:57]
    halfc = stats[:, 57:58]

    xs_col = [stats[:, 20 + i:21 + i] for i in range(NT)]
    ixs_col = [stats[:, 20 + NT + i:21 + NT + i] for i in range(NT)]
    xs_blk = stats[:, 20:20 + NT]
    assert 20 + 2 * NT <= 56

    assert 2 * TC >= D
    psQ = [nc.alloc_psum_tensor(f"psQ{i}", [128, TC], F32).ap() for i in range(2)]
    psA = nc.alloc_psum_tensor("psA", [128, D], BF16).ap()
    psB = nc.alloc_psum_tensor("psB", [128, TC], F32).ap()
    psBbf = psB.bitcast(BF16)[:, 0:D]
    psA2 = [psA, psQ[0].bitcast(BF16)[:, 0:D]]
    psB2 = [psBbf, psQ[1].bitcast(BF16)[:, 0:D]]

    assert HQ % 2 == 0
    chunks = [("q", c) for c in range(HQ)]
    for j in range(HQ // 2):
        chunks += [("k", 2 * j), ("k", 2 * j + 1), ("v", 2 * j), ("v", 2 * j + 1)]
    NCH = len(chunks)
    def pair_of(n):
        if n < HQ:
            return n // 2
        m_ = n - HQ
        return HQ // 2 + 2 * (m_ // 4) + (m_ % 4) // 2
    pair_col = []
    for p in range(3 * HQ // 2):
        if p < HQ // 2:
            pair_col.append(p * 256)
        else:
            g, r = divmod(p - HQ // 2, 2)
            pair_col.append((HQ * 128) * (1 + r) + g * 256)
    NP = len(pair_col)

    def jcol(kind, c):
        return {"q": 0, "k": HQ, "v": 2 * HQ}[kind] + c

    LA, LB = 2, 6
    order = []
    for s in range(FS + LB + 1):
        if s < FS:
            order.append(("quant", s))
        if LA <= s < FS + LA:
            order.append(("e1", s - LA))
        if s == LB - 1:
            order.append(("g", -1))
        if LB <= s < FS + LB:
            order.append(("e3", s - LB))
        if LB + 1 <= s:
            order.append(("e4", s - LB - 1))
    early_q = {}
    if HQ >= 2:
        p0 = max(0, HQ - 5)
        p1 = max(p0 + 1, HQ - 3)
        early_q = {p0: 0, p1: 1}
    early_mset = set(early_q.values())
    QNAMES = ["wmax", "wmaxc", "ws", "rcp", "winv", "tsw", "wqbf"]

    P = {}
    P["sxa"] = 4 * 16
    P["swr"] = [16 if m < 2 else 16 * ((m - 2) // 2 + 1) for m in range(FS)]
    def sw_val(p):
        if NSTG and p >= 6:
            return 16 * ((p - 6) // 2 + 2)
        return 16 * (p // 2 + 1)
    P["sw"] = [sw_val(pair_of(n)) for n in range(NCH)]
    P["scc_bounce"], P["scc_cc"], P["scc_ret"] = 16, 1, 16
    P["so"] = [16 * (m // 2 + 1) for m in range(FS)]

    v = 0
    v += 1; P["v_ones"] = v
    v += 1; P["v_onec"] = v
    v += 1; P["v_halfc"] = v
    pa_v = [dict() for _ in range(NT)]
    pb_v0 = [None] * max(EARLY, 1)
    for i in range(NT + 1):
        if i < NT:
            for nm in ["mx", "mn", "m", "m2", "a"]:
                v += 1; pa_v[i][nm] = v
        if i >= 1:
            for nm in ["nmr", "dhi", "dlo", "xmax", "xs", "xsc", "ixs", "tsq", "xqbf",
                       "xnbf"]:
                v += 1; pa_v[i - 1][nm] = v
            if OVL and i - 1 >= 6:
                for n0 in (2 * (i - 7), 2 * (i - 7) + 1):
                    if 0 <= n0 < EARLY:
                        v += 1; pb_v0[n0] = v
    for n0 in range(max(2 * (NT - 6), 0), EARLY):
        v += 1; pb_v0[n0] = v
    P["v_pa_end"] = v
    pf_v = [dict() for _ in range(FS)]
    pb_v = []
    vi = 0
    for (kind, c) in chunks:
        if kind == "v":
            v += 1; pb_v.append(v)
            if vi in early_q:
                for nm in QNAMES:
                    v += 1; pf_v[early_q[vi]][nm] = v
            vi += 1
        else:
            pb_v.append(None)
    P["v_pb_end"] = v
    for (what, m) in order:
        if what == "quant":
            if m in early_mset:
                continue
            for nm in QNAMES:
                v += 1; pf_v[m][nm] = v
        elif what == "g":
            v += 1; P["v_p"] = v
            v += 1; P["v_g"] = v
        else:
            v += 1; pf_v[m][what] = v
    P["v_end"] = v

    a = 0
    pa_a = [dict() for _ in range(NT)]
    for i in range(NT + 2):
        if i < NT:
            for nm in ["sum", "ssq"]:
                a += 1; pa_a[i][nm] = a
        if 1 <= i <= NT:
            for nm in ["lnp", "r", "xn"]:
                a += 1; pa_a[i - 1][nm] = a
        if i >= 2:
            for nm in ["evA", "evB"]:
                a += 1; pa_a[i - 2][nm] = a
    a += 1; P["a_xsrow"] = a
    pb_a = []
    for (kind, c) in chunks:
        if kind in ("q", "k"):
            a += 1; ev = a
            a += 1; pb_a.append({"evac": ev, "accum": a})
        else:
            a += 1; pb_a.append({"accum": a})
    P["a_pb_end"] = a
    pf_a = [dict() for _ in range(FS)]
    for (what, m) in order:
        if what == "quant":
            a += 1; pf_a[m]["evWa"] = a
            if SPLIT:
                a += 1; pf_a[m]["evWb"] = a
        elif what == "g":
            a += 1; P["a_lnp"] = a
            a += 1; P["a_rs"] = a
        elif what == "e1":
            a += 1; pf_a[m]["e2"] = a
    P["a_end"] = a

    t = 0
    pa_t = []
    pb_t0 = [None] * max(EARLY, 1)
    for i in range(NT):
        t += 1; g1 = t
        t += 1; pa_t.append((g1, t))
        if OVL and i >= 5 and (i - 5) < NE3:
            t += 1; pb_t0[i - 5] = t
    for n in range(NE3, EARLY):
        t += 1; pb_t0[n] = t
    t += 1; P["t_xsbc"] = t
    pb_t = []
    for _ in chunks:
        t += 1; pb_t.append(t)
    P["t_pb_end"] = t
    pf_t = [dict() for _ in range(FS)]
    torder = []
    for (what, m) in order:
        if what == "quant":
            torder.append(("twqA", m))
            if SPLIT:
                torder.append(("twqB", m))
            torder.append(("mmA", m))
            if SPLIT:
                torder.append(("mmB", m))
        elif what == "e3":
            torder.append(("tout", m))
    assert sorted(m for (w, m) in torder if w == "tout") == list(range(FS))
    for (what, m) in torder:
        t += 1; pf_t[m][what] = t
    P["t_end"] = t

    import contextlib
    es = contextlib.ExitStack()
    with es:
        sxa = es.enter_context(nc.semaphore("sxa"))
        sxs = es.enter_context(nc.semaphore("sxs"))
        sxt = [es.enter_context(nc.semaphore(f"sxt{i}")) for i in range(NT)]
        sw2 = [es.enter_context(nc.semaphore(f"sw{i}")) for i in range(2)]
        swr2 = [es.enter_context(nc.semaphore(f"swr{i}")) for i in range(2)]
        swrh2 = [es.enter_context(nc.semaphore(f"swrh{i}")) for i in range(2)]
        sstg = [es.enter_context(nc.semaphore(f"sstg{i}")) for i in range(max(NSTG, 1))]
        sxr2 = [es.enter_context(nc.semaphore(f"sxr{i}")) for i in range(2)]
        so2 = [es.enter_context(nc.semaphore(f"so{i}")) for i in range(2)]
        scb = es.enter_context(nc.semaphore("scb"))
        sccol = es.enter_context(nc.semaphore("sccol"))
        scret = es.enter_context(nc.semaphore("scret"))
        st = es.enter_context(nc.semaphore("st"))
        sv = es.enter_context(nc.semaphore("sv"))
        sa = es.enter_context(nc.semaphore("sa"))
        block = es.enter_context(nc.Block())

        def stepper(eng, self_sem):
            state = {"c": 0}

            def step(emit, *waits):
                for (s, val) in waits:
                    if val and val > 0:
                        eng.wait_ge(s, val)
                if state["c"] > 0:
                    eng.wait_ge(self_sem, state["c"])
                emit().then_inc(self_sem, 1)
                state["c"] += 1
                return state["c"]
            return step, state

        @block.sync
        def _(sync):
            sync.dma_start(out=battn[:, :], in_=bcol_h).then_inc(sxa, 16)
            sync.dma_start(out=biasc[:, :], in_=biascol_h).then_inc(sxa, 16)
            sync.dma_start(out=identb[:, :], in_=identb_h).then_inc(sxa, 16)
            sync.dma_start(out=identf[:, :], in_=identf_h).then_inc(sxa, 16)
            for i in range(NT):
                if i >= 3:
                    sync.wait_ge(sa, pa_a[i - 3]["xn"])
                    sync.wait_ge(sv, pa_v[i - 3]["mn"])
                sync.dma_start(out=xstage[i % 3][:, :],
                               in_=x_h[i * 128:(i + 1) * 128, :]).then_inc(sxt[i], 16)
            sync.wait_ge(sv, pa_v[NT - 1]["ixs"])
            sync.dma_start(out=xs_dram.ap().transpose([1, 0]), in_=xs_blk,
                           allow_slow_non_contiguous=True).then_inc(sxs, 16)
            sync.wait_ge(sxs, 16)
            sync.dma_start(out=xsrow1[:, :],
                           in_=xs_dram.ap().rearrange("j p -> (j p)").unsqueeze(0)).then_inc(sxs, 16)
            i2l = [i for i in range(NT) if i % 3 == 2]
            if i2l:
                sync.wait_ge(sa, pa_a[i2l[-1]]["xn"])
                sync.wait_ge(sv, pa_v[i2l[-1]]["mn"])
            for m in range(min(2, FS)):
                if NSTG and m == 1:
                    sync.wait_ge(st, pb_t[7])
                sync.dma_start(out=wrow[m % 2][:, :],
                               in_=wt_h[m * 128:(m + 1) * 128, :]).then_inc(swrh2[m % 2], 16)

            def xre_dma(m):
                srcap = x_h[:, m * 128:(m + 1) * 128].rearrange("(i p) c -> p i c", p=128)
                dstap = xre[m % 2].rearrange("p (i c) -> p i c", i=NT)
                sync.dma_start(out=dstap, in_=srcap).then_inc(sxr2[m % 2], 16)
            if NSTG:
                sync.wait_ge(st, pb_t[11])
            for m in range(min(2, FS)):
                xre_dma(m)
            sync.wait_ge(sccol, P["scc_cc"])
            sync.dma_start(out=Sglob[:, :], in_=cc_out.ap()).then_inc(scret, 16)
            for m in range(FS):
                sync.wait_ge(sv, pf_v[m]["e4"])
                if m + 2 < FS:
                    xre_dma(m + 2)
                dst = out_h.rearrange("(i p) c -> p i c", p=128)[:, :, m * 128:(m + 1) * 128]
                srcap = outb[m % 2].rearrange("p (i c) -> p i c", i=NT)
                sync.dma_start(out=dst, in_=srcap).then_inc(so2[m % 2], 16)
            sync.wait_ge(so2[0], 16 * ((FS + 1) // 2))
            sync.wait_ge(so2[1], 16 * (FS // 2))
            if debug:
                views = dict(xnT=xnT, xqT=xqT, qT=qT, Sloc=Sloc, Sglob=Sglob,
                             g=g_sb, xsrow=xsrow, wscol=wscol, stats=stats)
                ndbg = 0
                for nm, ap in dbg_h.items():
                    sync.dma_start(out=ap, in_=views[nm]).then_inc(sxs, 16)
                    ndbg += 1
                sync.wait_ge(sxs, 32 + 16 * ndbg)

        @block.gpsimd
        def _(gpsimd):
            def wdma(p):
                o0 = pair_col[p]
                srcap = W_full[:, o0:o0 + 256].rearrange("(k q) c -> q k c", q=128)
                dst = wbuf[p % 2].rearrange("p (k c) -> p k c", k=KC)
                gpsimd.dma_start(out=dst, in_=srcap).then_inc(sw2[p % 2], 16)
            wdma(0)
            wdma(1)

            def stgdma(j):
                o0 = pair_col[2 + j]
                srcap = W_full[:, o0:o0 + 256].rearrange("(k q) c -> q k c", q=128)
                dst = stg[j].rearrange("p (k c) -> p k c", k=KC)
                gpsimd.dma_start(out=dst, in_=srcap).then_inc(sstg[j], 16)
            if NSTG:
                stgdma(0)
                stgdma(1)
                stgdma(3)
                gpsimd.wait_ge(sa, pa_a[6]["xn"])
                gpsimd.wait_ge(sv, pa_v[6]["mn"])
                stgdma(2)
            last_chunk_of_pair = {}
            for n in range(NCH):
                last_chunk_of_pair[pair_of(n)] = n
            for p in range(2 if not NSTG else 6, NP):
                if NSTG and p in (6, 7):
                    gpsimd.wait_ge(st, pb_t[last_chunk_of_pair[p - 6]])
                else:
                    gpsimd.wait_ge(st, pb_t[last_chunk_of_pair[p - 2]])
                wdma(p)
            gpsimd.wait_ge(sv, P["v_pb_end"])
            gpsimd.wait_ge(sa, P["a_pb_end"])
            gpsimd.dma_start(out=cc_in.ap(), in_=Sloc[:, :]).then_inc(scb, 16)
            gpsimd.wait_ge(scb, 16)
            if use_cc:
                gpsimd.collective_compute(
                    "AllReduce", ALU.add, replica_groups=[list(range(n_cores))],
                    ins=[cc_in.ap().opt()], outs=[cc_out.ap().opt()],
                ).then_inc(sccol, 1)
            else:
                gpsimd.dma_start(out=cc_out.ap(), in_=cc_in.ap()).then_inc(sccol, 1)
            for m in range(FS - 2):
                m2 = m + 2
                gpsimd.wait_ge(sv, pf_v[m]["tsw"])
                gpsimd.dma_start(out=wrow[m2 % 2][:, :],
                                 in_=wt_h[m2 * 128:(m2 + 1) * 128, :]).then_inc(swr2[m2 % 2], 16)

        @block.scalar
        def _(scalar):
            step, _sa8 = stepper(scalar, sa)
            scalar.wait_ge(sxa, P["sxa"])
            for i in range(NT + 2):
                if i < NT:
                    xi = xstage[i % 3]
                    step(lambda i=i, xi=xi: scalar.activation(
                            junk2[:, 0:D], xi[:, :], AF.Identity,
                            accum_out=stc(C_SUM, i)),
                         (sxt[i], 16),
                         (sv, pa_v[i - 2]["m"] if i > 1 else 0))
                    step(lambda i=i, xi=xi: scalar.activation(
                            junk2[:, 0:D], xi[:, :], AF.Square,
                            accum_out=stc(C_SSQ, i)),
                         (sv, pa_v[i - 2]["a"] if i > 1 else 0))
                if 1 <= i <= NT:
                    j = i - 1
                    xj = xstage[j % 3]
                    step(lambda j=j: scalar.activation(stc(C_R, j), stc(C_A, j), AF.Ln,
                                                       bias=halfc),
                         (sv, pa_v[j]["a"]),
                         (sv, P["v_halfc"]),
                         (sv, pa_v[j - 2]["nmr"] if j > 1 else 0))
                    step(lambda j=j: scalar.activation(stc(C_R, j), stc(C_R, j),
                                                       AF.Exp, scale=-0.5))
                    step(lambda j=j, xj=xj: scalar.activation(
                            scr0[j % 2][:, :], xj[:, :], AF.Identity,
                            scale=stc(C_R, j), bias=stc(C_NMR, j)),
                         (sv, pa_v[j]["nmr"]),
                         (sv, pa_v[j - 2]["xnbf"] if j > 1 else 0))
                if i >= 2:
                    jj = i - 2
                    pA = psA2[jj % 2] if (not OVL or jj < 4) else psA
                    pB = psB2[jj % 2] if (not OVL or jj < 4) else psBbf
                    step(lambda jj=jj, pA=pA: scalar.activation(
                            xnT.rearrange("p (k t) -> p k t", k=KC)[:, :, jj * 128:(jj + 1) * 128],
                            pA.rearrange("p (k t) -> p k t", k=KC), AF.Identity),
                         (st, pa_t[jj][0]))
                    step(lambda jj=jj, pB=pB: scalar.activation(
                            xqT.rearrange("p (k t) -> p k t", k=KC)[:, :, jj * 128:(jj + 1) * 128],
                            pB.rearrange("p (k t) -> p k t", k=KC), AF.Identity),
                         (st, pa_t[jj][1]))
            step(lambda: scalar.activation(xsrow[:, :], psB[:, 0:TC], AF.Identity),
                 (st, P["t_xsbc"]))
            last_v_par = {0: 0, 1: 0}
            for n, (kind, c) in enumerate(chunks):
                j = jcol(kind, c)
                if kind == "q":
                    if n < EARLY:
                        step(lambda n=n, c=c, j=j: scalar.activation(
                                qT[:, c * TC + NH:(c + 1) * TC], psQ[n % 2][:, NH:TC],
                                AF.Identity, bias=battn[:, j:j + 1], scale=onec),
                             (st, pb_t[n]),
                             (sv, pb_v0[n]))
                    else:
                        step(lambda n=n, c=c, j=j: scalar.activation(
                                qT[:, c * TC:(c + 1) * TC], psQ[n % 2][:, :],
                                AF.Identity, bias=battn[:, j:j + 1], scale=onec),
                             (st, pb_t[n]),
                             (sv, P["v_onec"]))
                    step(lambda c=c: scalar.activation(
                            junk2[:, 0:TC], qT[:, c * TC:(c + 1) * TC], AF.Square,
                            accum_out=Sloc[:, c:c + 1]))
                elif kind == "k":
                    pk = c % 2
                    step(lambda n=n, j=j, pk=pk: scalar.activation(
                            scr1[:, pk * TC:(pk + 1) * TC],
                            psQ[n % 2][:, :], AF.Identity, bias=battn[:, j:j + 1],
                            scale=onec),
                         (st, pb_t[n]),
                         (sv, pb_v[last_v_par[pk]] if last_v_par[pk] else 0),
                         (sv, pa_v[NT - 1]["xqbf"] if last_v_par[pk] == 0 else 0))
                    step(lambda c=c, pk=pk: scalar.activation(
                            junk2[:, 0:TC], scr1[:, pk * TC:(pk + 1) * TC], AF.Square,
                            accum_out=Sloc[:, HQ + c:HQ + c + 1]))
                else:
                    last_v_par[c % 2] = n
                    step(lambda c=c: scalar.activation(
                            junk2[:, 0:TC], junk[:, 0:TC], AF.Identity,
                            accum_out=Sloc[:, 2 * HQ + c:2 * HQ + c + 1]),
                         (sv, pb_v[n]))
            for (what, m) in order:
                if what == "quant":
                    step(lambda m=m: scalar.activation(
                            wqTb[m % 2][:, 0:KH * 128], psA[:, 0:KH * 128],
                            AF.Identity),
                         (st, pf_t[m]["twqA"]),
                         (st, pf_t[m - 2]["mmA"] if m > 1 else 0))
                    if SPLIT:
                        step(lambda m=m: scalar.activation(
                                wqTb[m % 2][:, KH * 128:KC * 128],
                                psA[:, KH * 128:KC * 128], AF.Identity),
                             (st, pf_t[m]["twqB"]),
                             (st, pf_t[m - 2]["mmB"] if m > 1 else 0))
                elif what == "g":
                    step(lambda: scalar.activation(gtmp[:, :], gtmp[:, :], AF.Ln),
                         (sv, P["v_p"]))
                    step(lambda: scalar.activation(gtmp[:, :], gtmp[:, :], AF.Exp,
                                                   scale=-0.5))
                elif what == "e1":
                    step(lambda m=m: scalar.activation(
                            scrU2[m % 4][:, :], scrU[m % 2][:, :], AF.Identity,
                            scale=wscol[:, m:m + 1], bias=biasc[:, m:m + 1]),
                         (sv, pf_v[m]["e1"]),
                         (sv, pf_v[m - 4]["e3"] if m > 3 else 0))
            assert _sa8["c"] == P["a_end"]

        @block.vector
        def _(vector):
            step, _sv8 = stepper(vector, sv)
            vector.wait_ge(sxa, P["sxa"])
            step(lambda: vector.memset(ones_sb[:, :], 1.0))
            step(lambda: vector.memset(onec, 1.0))
            step(lambda: vector.memset(halfc, eps))
            invD = 1.0 / D
            C_MX, C_MN = C_SSQ, C_M2
            for i in range(NT + 1):
                if i < NT:
                    xi = xstage[i % 3]
                    step(lambda i=i, xi=xi: vector.reduce_max(stats[:, 16 + (i & 1):17 + (i & 1)],
                                                              xi[:, :], axis=AX),
                         (sxt[i], 16),
                         (sv, pa_v[i - 2]["dhi"] if i > 1 else 0))
                    step(lambda i=i, xi=xi: vector.tensor_reduce(
                            stats[:, 18 + (i & 1):19 + (i & 1)], xi[:, :],
                            axis=AX, op=ALU.min),
                         (sv, pa_v[i - 2]["dlo"] if i > 1 else 0))
                    step(lambda i=i: vector.tensor_scalar_mul(stc(C_M, i),
                                                              stc(C_SUM, i), invD),
                         (sa, pa_a[i]["sum"]),
                         (sv, pa_v[i - 2]["nmr"] if i > 1 else 0))
                    step(lambda i=i: vector.tensor_mul(stc(C_M2, i), stc(C_M, i),
                                                       stc(C_M, i)))
                    step(lambda i=i: vector.scalar_tensor_tensor(
                            stc(C_A, i), stc(C_SSQ, i), invD, stc(C_M2, i),
                            ALU.mult, ALU.subtract),
                         (sa, pa_a[i]["ssq"]),
                         (sa, pa_a[i - 2]["lnp"] if i > 1 else 0))
                if i >= 1:
                    j = i - 1
                    step(lambda j=j: vector.tensor_scalar(stc(C_NMR, j), stc(C_M, j),
                                                          stc(C_R, j), -1.0,
                                                          ALU.mult, ALU.mult),
                         (sa, pa_a[j]["r"]),
                         (sa, pa_a[j - 2]["xn"] if j > 1 else 0))
                    step(lambda j=j: vector.tensor_sub(
                            stats[:, 16 + (j & 1):17 + (j & 1)],
                            stats[:, 16 + (j & 1):17 + (j & 1)], stc(C_M, j)))
                    step(lambda j=j: vector.tensor_sub(
                            stats[:, 18 + (j & 1):19 + (j & 1)],
                            stc(C_M, j), stats[:, 18 + (j & 1):19 + (j & 1)]))
                    step(lambda j=j: vector.tensor_max(
                            stc(C_XMAX, j),
                            stats[:, 16 + (j & 1):17 + (j & 1)],
                            stats[:, 18 + (j & 1):19 + (j & 1)]))
                    step(lambda j=j: vector.scalar_tensor_tensor(
                            xs_col[j], stc(C_XMAX, j), 1.0 / 127.0, stc(C_R, j),
                            ALU.mult, ALU.mult),
                         (sa, pa_a[j]["r"]))
                    step(lambda j=j: vector.tensor_scalar_max(xs_col[j], xs_col[j],
                                                              1e-12))
                    step(lambda j=j: vector.reciprocal(ixs_col[j], xs_col[j]))
                    step(lambda j=j: vector.tensor_scalar_mul(scr1_i32[:, :],
                                                              scr0[j % 2][:, :],
                                                              ixs_col[j]),
                         (sa, pa_a[j]["xn"]))
                    step(lambda: vector.tensor_copy(xqbf[:, :], scr1_i32[:, :]),
                         (st, pa_t[j - 1][1] if j > 0 else 0))
                    step(lambda j=j: vector.tensor_copy(xnbf[:, :], scr0[j % 2][:, :]),
                         (sa, pa_a[j]["xn"]),
                         (st, pa_t[j - 1][0] if j > 0 else 0))
                    if OVL and j >= 6:
                        for n0 in (2 * (j - 6), 2 * (j - 6) + 1):
                            if 0 <= n0 < EARLY:
                                step(lambda n0=n0: vector.tensor_scalar(
                                        qT[:, n0 * TC:n0 * TC + NH],
                                        psQ[n0 % 2][:, 0:NH],
                                        battn[:, n0:n0 + 1], None, ALU.add),
                                     (st, pb_t0[n0]))
            for n0 in range(max(2 * (NT - 6), 0), EARLY):
                step(lambda n0=n0: vector.tensor_scalar(
                        qT[:, n0 * TC:n0 * TC + NH], psQ[n0 % 2][:, 0:NH],
                        battn[:, n0:n0 + 1], None, ALU.add),
                     (st, pb_t0[n0]))
            assert _sv8["c"] == P["v_pa_end"]

            def emit_quant(m):
                step(lambda m=m: vector.reduce_max(stc(C_XMAX, m), wrow[m % 2][:, :],
                                                   axis=AX, apply_absolute_value=True),
                     ((swrh2 if m < 2 else swr2)[m % 2], P["swr"][m]))
                step(lambda m=m: vector.tensor_scalar_max(stc(C_SUM, m),
                                                          stc(C_XMAX, m), 1e-8))
                step(lambda m=m: vector.tensor_scalar_mul(wscol[:, m:m + 1],
                                                          stc(C_SUM, m), 1.0 / 127.0))
                step(lambda m=m: vector.reciprocal(stc(C_M, m), stc(C_SUM, m)))
                step(lambda m=m: vector.tensor_scalar_mul(winv[:, m:m + 1],
                                                          stc(C_M, m), 127.0))
                step(lambda m=m: vector.tensor_scalar_mul(scr0_i32[:, :],
                                                          wrow[m % 2][:, :],
                                                          winv[:, m:m + 1]))
                step(lambda m=m: vector.tensor_copy(wqbf2[m % 2][:, :], scr0_i32[:, :]),
                     (st, pf_t[m - 2][TWQL] if m > 1 else
                          (pa_t[NT - 1][1] if m == 0 else pa_t[NT - 1][0])))
            prev_v = 0
            vi = 0
            kidx = {}
            for n, (kind, c) in enumerate(chunks):
                if kind == "k":
                    kidx[c] = n
                if kind != "v":
                    continue
                j = jcol(kind, c)
                pk = c % 2
                step(lambda n=n, j=j, pk=pk: vector.scalar_tensor_tensor(
                        junk[:, 0:TC], psQ[n % 2][:, :], battn[:, j:j + 1],
                        scr1[:, pk * TC:(pk + 1) * TC], ALU.add, ALU.mult),
                     (st, pb_t[n]),
                     (sa, pb_a[kidx[c]]["evac"]),
                     (sa, pb_a[prev_v]["accum"] if prev_v else 0))
                prev_v = n
                if vi in early_q:
                    emit_quant(early_q[vi])
                vi += 1
            assert _sv8["c"] == P["v_pb_end"]
            for (what, m) in order:
                if what == "quant":
                    if m not in early_mset:
                        emit_quant(m)
                elif what == "g":
                    step(lambda: vector.tensor_mul(gtmp[:, :], Sglob[:, 0:HQ],
                                                   Sglob[:, HQ:2 * HQ]),
                         (scret, P["scc_ret"]))
                    step(lambda: vector.tensor_mul(g_sb[:, :], Sglob[:, 2 * HQ:3 * HQ],
                                                   gtmp[:, :]),
                         (sa, P["a_rs"]))
                elif what == "e1":
                    step(lambda m=m: vector.tensor_mul(scrU[m % 2][:, :],
                                                       psQ[m % 2][:, :], xsrow[:, :]),
                         (st, pf_t[m][MML]),
                         (sa, P["a_xsrow"] if m == 0 else 0),
                         (sa, pf_a[m - 2]["e2"] if m > 1 else 0))
                elif what == "e3":
                    step(lambda m=m: vector.scalar_tensor_tensor(
                            scrO[m % 2][:, :], qT[:, m * TC:(m + 1) * TC],
                            g_sb[:, m:m + 1],
                            scrU2[m % 4][:, :], ALU.mult, ALU.add),
                         (sa, pf_a[m]["e2"]),
                         (st, pf_t[m - 2]["tout"] if m > 1 else 0))
                else:
                    step(lambda m=m: vector.tensor_tensor(
                            outb[m % 2][:, :], psB[:, 0:TC], xre[m % 2][:, :],
                            ALU.add),
                         (st, pf_t[m]["tout"]),
                         (sxr2[m % 2], 16 * (m // 2 + 1)),
                         (so2[m % 2], P["so"][m - 2] if m > 1 else 0))
            assert _sv8["c"] == P["v_end"]

        @block.tensor
        def _(tensor):
            tcnt = {"c": 0}

            def group(emits, *waits):
                for (s, val) in waits:
                    if val and val > 0:
                        tensor.wait_ge(s, val)
                if tcnt["c"] > 0:
                    tensor.wait_ge(st, tcnt["c"])
                last = None
                for e in emits:
                    last = e()
                last.then_inc(st, 1)
                tcnt["c"] += 1
                return tcnt["c"]

            def lhsT_of(n, k):
                p = pair_of(n)
                sub = (jcol(chunks[n][0], chunks[n][1]) * 128 - pair_col[p]) // 128
                if NSTG and 4 <= n < EARLY:
                    buf = stg[p - 2]
                else:
                    buf = wbuf[p % 2]
                return buf.rearrange("p (k c) -> p k c", k=KC)[:, k, sub * 128:(sub + 1) * 128]

            def emit_h0(n):
                p = pair_of(n)
                if NSTG and 4 <= n < EARLY:
                    waits = [(sstg[p - 2], 16)]
                else:
                    waits = [(sw2[p % 2], P["sw"][n])]
                if n >= 2:
                    waits.append((sv, pb_v0[n - 2]))
                elif n == 0:
                    waits.append((sa, pa_a[3]["evA"]))
                else:
                    waits.append((sa, pa_a[3]["evB"]))
                group([lambda n=n, k=k: tensor.matmul(
                        psQ[n % 2][:, 0:NH], lhsT_of(n, k),
                        xnT[:, k * TC: k * TC + NH],
                        start=(k == 0), stop=(k == KC - 1)) for k in range(KC)],
                      *waits)

            for i in range(NT):
                pA = psA2[i % 2] if (not OVL or i < 4) else psA
                pB = psB2[i % 2] if (not OVL or i < 4) else psBbf
                if not OVL or i < 4:
                    wA = (sa, pa_a[i - 2]["evA"] if i > 1 else 0)
                    wB = (sa, pa_a[i - 2]["evB"] if i > 1 else 0)
                else:
                    wA = (sa, pa_a[i - 1 if i > 4 else 2]["evA"])
                    wB = (sa, pa_a[i - 1 if i > 4 else 2]["evB"])
                group([lambda k=k, pA=pA: tensor.transpose(
                            pA[:, k * 128:(k + 1) * 128],
                            xnbf[:, k * 128:(k + 1) * 128],
                            identb[:, :]) for k in range(KC)],
                      (sxa, P["sxa"]),
                      (sv, pa_v[i]["xnbf"]),
                      wA)
                group([lambda k=k, pB=pB: tensor.transpose(
                            pB[:, k * 128:(k + 1) * 128],
                            xqbf[:, k * 128:(k + 1) * 128],
                            identb[:, :]) for k in range(KC)],
                      (sv, pa_v[i]["xqbf"]),
                      wB)
                if OVL and i >= 5 and (i - 5) < NE3:
                    emit_h0(i - 5)
            for n in range(NE3, EARLY):
                emit_h0(n)
            group([(lambda h=h: tensor.matmul(psB[:, h * NH:(h + 1) * NH], ones_sb[:, :],
                                              xsrow1[:, h * NH:(h + 1) * NH],
                                              start=True, stop=True)) for h in range(HL)],
                  (sxs, 32),
                  (sv, P["v_ones"]),
                  (sa, pa_a[NT - 1]["evB"]))
            for n, (kind, c) in enumerate(chunks):
                waits = []
                if n < EARLY:
                    if n == 0:
                        waits.append((sa, pa_a[NT - 1]["evA"]))
                    if n >= 2:
                        waits.append((sa, pb_a[n - 2]["evac"]))
                    group([lambda n=n, k=k: tensor.matmul(
                            psQ[n % 2][:, NH:TC], lhsT_of(n, k),
                            xnT[:, k * TC + NH: k * TC + TC],
                            start=(k == 0), stop=(k == KC - 1)) for k in range(KC)],
                          *waits)
                    continue
                waits = [(sw2[pair_of(n) % 2], P["sw"][n])]
                if n == 0:
                    waits.append((sa, pa_a[NT - 1]["evB"]))
                if n >= 2:
                    k2, _ = chunks[n - 2]
                    if k2 == "v":
                        waits.append((sv, pb_v[n - 2]))
                    else:
                        waits.append((sa, pb_a[n - 2]["evac"]))
                emits = []
                for k in range(KC):
                    for h in range(HL):
                        emits.append(lambda n=n, k=k, h=h: tensor.matmul(
                            psQ[n % 2][:, h * NH:(h + 1) * NH], lhsT_of(n, k),
                            xnT[:, k * TC + h * NH: k * TC + (h + 1) * NH],
                            start=(k == 0), stop=(k == KC - 1)))
                group(emits, *waits)
            for (what, m) in torder:
                if what == "twqA":
                    waits = [(sv, pf_v[m]["wqbf"]),
                             (sa, pf_a[m - 1]["evWa"] if m > 0 else 0)]
                    if m == 0:
                        waits.append((sa, pa_a[NT - 1]["evA"]))
                    group([lambda k=k, m=m: tensor.transpose(
                            psA[:, k * 128:(k + 1) * 128],
                            wqbf2[m % 2][:, k * 128:(k + 1) * 128],
                            identb[:, :]) for k in range(KH)],
                          *waits)
                elif what == "twqB":
                    group([lambda k=k, m=m: tensor.transpose(
                            psA[:, k * 128:(k + 1) * 128],
                            wqbf2[m % 2][:, k * 128:(k + 1) * 128],
                            identb[:, :]) for k in range(KH, KC)],
                          (sa, pf_a[m - 1]["evWb"] if m > 0 else 0))
                elif what in ("mmA", "mmB"):
                    ka, kb = (0, KH) if what == "mmA" else (KH, KC)
                    if what == "mmA":
                        waits = [(sa, pf_a[m]["evWa"])]
                        if m >= 2:
                            waits.append((sv, pf_v[m - 2]["e1"]))
                        else:
                            waits.append((sv, pb_v[NCH - 2 + m]))
                    else:
                        waits = [(sa, pf_a[m]["evWb"])]
                    emits = []
                    for k in range(ka, kb):
                        lhsT = wqTb[m % 2].rearrange("p (k c) -> p k c", k=KC)[:, k, :]
                        for h in range(HL):
                            emits.append(lambda m=m, k=k, h=h, lhsT=lhsT: tensor.matmul(
                                psQ[m % 2][:, h * NH:(h + 1) * NH], lhsT,
                                xqT[:, k * TC + h * NH: k * TC + (h + 1) * NH],
                                start=(k == 0), stop=(k == KC - 1)))
                    group(emits, *waits)
                else:
                    waits = [(sv, pf_v[m]["e3"])]
                    if m == 0:
                        waits.append((sa, P["a_xsrow"]))
                    else:
                        waits.append((sv, pf_v[m - 1]["e4"]))
                    group([lambda i=i: tensor.transpose(
                            psB[:, i * 128:(i + 1) * 128],
                            scrO[m % 2][:, i * 128:(i + 1) * 128],
                            identf[:, :]) for i in range(NT)],
                          *waits)
            assert tcnt["c"] == P["t_end"]

    meta = dict(D=D, TC=TC, NT=NT, KC=KC, HQ=HQ, OC=OC, FS=FS)
    return nc, meta


def _prep_inputs(x, W, b, wt, bias, NT, KC, HQ, n_cores=N_CORES):
    import ml_dtypes
    D = 128 * KC
    TC = 128 * NT
    OC = 3 * HQ
    FS = HQ
    x2 = np.ascontiguousarray(np.asarray(x, dtype=np.float32).reshape(n_cores * TC, D))
    W = np.ascontiguousarray(np.asarray(W, dtype=np.float32))
    wt = np.ascontiguousarray(np.asarray(wt, dtype=np.float32))
    bcol = np.ascontiguousarray(np.asarray(b, dtype=np.float32).reshape(OC, 128).T)
    biascol = np.ascontiguousarray(np.asarray(bias, dtype=np.float32).reshape(FS, 128).T)
    identb = np.ascontiguousarray(np.eye(128, dtype=ml_dtypes.bfloat16))
    identf = np.ascontiguousarray(np.eye(128, dtype=np.float32))
    in_maps = []
    for cid in range(n_cores):
        in_maps.append({
            "x": x2[cid * TC:(cid + 1) * TC],
            "W": W,
            "wt": wt, "bcol": bcol, "biascol": biascol,
            "identb": identb, "identf": identf,
        })
    return in_maps


def kernel(L0_attn_x, L0_attn_W, L0_attn_b, L0_ffn_weight, L0_ffn_bias):
    NT, KC, HQ = 8, 16, 16
    nc, meta = build_nc(NT, KC, HQ)
    in_maps = _prep_inputs(L0_attn_x, L0_attn_W, L0_attn_b,
                           L0_ffn_weight, L0_ffn_bias, NT, KC, HQ)
    res = run_bass_kernel_spmd(nc, in_maps, core_ids=list(range(N_CORES)))
    out = np.concatenate([np.asarray(r["out"]) for r in res.results], axis=0)
    return out.reshape(1, 8192, 2048).astype(np.float32)


# version 20 (final)
# speedup vs baseline: 1.0280x; 1.0280x over previous
import numpy as np
import concourse.bass as bass
import concourse.mybir as mybir
from concourse.bass_utils import run_bass_kernel_spmd

F32 = mybir.dt.float32
BF16 = mybir.dt.bfloat16
I32 = mybir.dt.int32
U8 = mybir.dt.uint8
AF = mybir.ActivationFunctionType
ALU = mybir.AluOpType
AX = mybir.AxisListType.X

N_CORES = 8


def build_nc(NT, KC, HQ, n_cores=N_CORES, use_cc=True, debug=False):
    D = 128 * KC
    TC = 128 * NT
    HL = 1 if TC <= 512 else 2
    NH = TC // HL
    assert NH <= 512
    OC = 3 * HQ
    FS = HQ
    SPLIT = (KC // 2) % 8 == 0
    KH = KC // 2 if SPLIT else KC
    TWQL = "twqB" if SPLIT else "twqA"
    MML = "mmB" if SPLIT else "mmA"

    OVL = (HL == 2 and NT == 8)
    EARLY = (12 if HQ >= 16 else min(4, HQ)) if OVL else 0
    NSTG = max(0, (EARLY - 4) // 2)
    NE3 = min(3, EARLY)

    assert FS >= 2
    eps = 1e-5

    nc = bass.Bass("TRN2", target_bir_lowering=False)

    x_h = nc.dram_tensor("x", [TC, D], F32, kind="ExternalInput").ap()
    W_full = nc.dram_tensor("W", [D, 3 * D], F32, kind="ExternalInput").ap()
    wt_h = nc.dram_tensor("wt", [D, D], F32, kind="ExternalInput").ap()
    bcol_h = nc.dram_tensor("bcol", [128, OC], F32, kind="ExternalInput").ap()
    biascol_h = nc.dram_tensor("biascol", [128, FS], F32, kind="ExternalInput").ap()
    identb_h = nc.dram_tensor("identb", [128, 128], BF16, kind="ExternalInput").ap()
    identf_h = nc.dram_tensor("identf", [128, 128], F32, kind="ExternalInput").ap()
    out_h = nc.dram_tensor("out", [TC, D], F32, kind="ExternalOutput").ap()
    dbg_h = {}
    if debug:
        for nm, rows, nel, dt in [("xnT", 128, KC * TC, BF16), ("xqT", 128, KC * TC, BF16),
                                  ("qT", 128, HQ * TC, BF16), ("Sloc", 128, 3 * HQ, F32),
                                  ("Sglob", 128, 3 * HQ, F32), ("g", 128, HQ, F32),
                                  ("xsrow", 128, TC, F32), ("wscol", 128, HQ, F32),
                                  ("stats", 128, 64, F32)]:
            dbg_h[nm] = nc.dram_tensor("dbg_" + nm, [rows, nel], dt, kind="ExternalOutput").ap()
    cc_in = nc.dram_tensor("cc_in", [128, 3 * HQ], F32)
    cc_out = nc.dram_tensor("cc_out", [128, 3 * HQ], F32)

    xs_dram = nc.dram_tensor("xs_dram", [NT, 128], F32)

    off = {}
    cur = 0

    def place(name, nbytes, align=32):
        nonlocal cur
        cur = (cur + align - 1) // align * align
        off[name] = cur
        cur += nbytes

    place("xstage", 2 * D * 4)
    place("xre", 2 * NT * 128 * 4)
    place("xnT", KC * TC * 2)
    place("xqT", KC * TC * 2)
    place("qT", HQ * TC * 2)
    place("scr0", 2 * D * 4)
    place("scr1", max(D, 2 * TC) * 4)
    place("junk", TC * 4)
    place("junk2", max(D, TC) * 4)
    place("xnbf", D * 2)
    place("xqbf", D * 2)
    place("wbuf", 2 * KC * 256 * 2)
    place("wrow", 2 * D * 4)
    place("xsrow", TC * 4)
    place("stats", 64 * 4)
    place("Sloc", 3 * HQ * 4)
    place("Sglob", 3 * HQ * 4)
    place("gtmp", HQ * 4)
    place("g", HQ * 4)
    place("wscol", HQ * 4)
    place("winv", HQ * 4)
    place("battn", OC * 4)
    place("biasc", FS * 4)
    place("identb", 128 * 2)
    place("identf", 128 * 4)
    place("ones", 128 * 4)
    if 2 * KC * 128 * 2 + 2 * TC * 4 + 2 * TC * 4 <= KC * TC * 2:
        off["wqTb"] = off["xnT"]
        off["outb"] = off["wqTb"] + 2 * KC * 128 * 2
        off["scrO"] = off["outb"] + 2 * TC * 4
    else:
        place("wqTb", 2 * KC * 128 * 2)
        place("outb", 2 * TC * 4)
        place("scrO", 2 * TC * 4)
    if 4 * TC * 4 <= 2 * KC * 256 * 2:
        off["scrU"] = off["wbuf"]
        off["scrU2"] = off["wbuf"] + 2 * TC * 4
    else:
        place("scrU", 2 * TC * 4)
        place("scrU2", 2 * TC * 4)
    off["scrU2hi"] = off["scr1"]
    off["xsrow1"] = off["xsrow"]
    arena_bytes = (cur + 31) // 32 * 32
    assert arena_bytes <= 206 * 1024, f"sbuf over budget: {arena_bytes}"

    arena = nc.alloc_sbuf_tensor("arena", [128, arena_bytes], U8).ap()
    ESZ = {F32: 4, BF16: 2, I32: 4}

    def view(name, dt, nelem, rows=128, byte_off=0):
        o = off[name] + byte_off
        return arena[0:rows, o:o + nelem * ESZ[dt]].bitcast(dt)

    xstage = [view("xstage", F32, D, byte_off=0),
              view("xstage", F32, D, byte_off=D * 4),
              view("wrow", F32, D, byte_off=0)]
    xre = [view("xre", F32, NT * 128, byte_off=i * NT * 128 * 4) for i in range(2)]
    xnT = view("xnT", BF16, KC * TC)
    xqT = view("xqT", BF16, KC * TC)
    qT = view("qT", BF16, HQ * TC)
    scr0 = [view("scr0", F32, D, byte_off=i * D * 4) for i in range(2)]
    scr0_i32 = view("scr0", I32, D)
    scr1 = view("scr1", F32, max(D, 2 * TC))
    junk = view("junk", F32, TC)
    junk2 = view("junk2", F32, max(D, TC))
    scr1_i32 = view("scr1", I32, D)
    xnbf = view("xnbf", BF16, D)
    xqbf = view("xqbf", BF16, D)
    wqbf2 = [xqbf, xnbf]
    wbuf = [view("wbuf", BF16, KC * 256, byte_off=i * KC * 256 * 2) for i in range(2)]
    wrow = [view("wrow", F32, D, byte_off=i * D * 4) for i in range(2)]
    wqTb = [view("wqTb", BF16, KC * 128, byte_off=i * KC * 128 * 2) for i in range(2)]
    if NSTG:
        assert NSTG == 4 and (12 + 4) * TC * 2 <= HQ * TC * 2
        assert KC * 256 * 2 <= min(D * 4, 2 * NT * 128 * 4)
        stg = [view("qT", BF16, KC * 256, byte_off=12 * TC * 2),
               view("wrow", BF16, KC * 256, byte_off=D * 4),
               view("xstage", BF16, KC * 256, byte_off=0),
               view("xre", BF16, KC * 256, byte_off=0)]
    else:
        stg = None
    scrU = [view("scrU", F32, TC, byte_off=i * TC * 4) for i in range(2)]
    scrU2 = [view("scrU2", F32, TC, byte_off=i * TC * 4) for i in range(2)] + \
            [view("scrU2hi", F32, TC, byte_off=i * TC * 4) for i in range(2)]
    scrO = [view("scrO", F32, TC, byte_off=i * TC * 4) for i in range(2)]
    outb = [view("outb", F32, TC, byte_off=i * TC * 4) for i in range(2)]
    xsrow = view("xsrow", F32, TC)
    xsrow1 = view("xsrow1", F32, TC, rows=1)
    stats = view("stats", F32, 64)
    Sloc = view("Sloc", F32, 3 * HQ)
    Sglob = view("Sglob", F32, 3 * HQ)
    gtmp = view("gtmp", F32, HQ)
    g_sb = view("g", F32, HQ)
    wscol = view("wscol", F32, HQ)
    winv = view("winv", F32, HQ)
    battn = view("battn", F32, OC)
    biasc = view("biasc", F32, FS)
    identb = view("identb", BF16, 128)
    identf = view("identf", F32, 128)
    ones_sb = view("ones", F32, 128, rows=1)

    C_SUM, C_SSQ, C_M, C_M2, C_A, C_R, C_NMR, C_XMAX = range(8)

    def stc(c, i=0):
        o = 2 * c + (i & 1)
        return stats[:, o:o + 1]

    onec = stats[:, 56:57]
    halfc = stats[:, 57:58]

    xs_col = [stats[:, 20 + i:21 + i] for i in range(NT)]
    ixs_col = [stats[:, 20 + NT + i:21 + NT + i] for i in range(NT)]
    xs_blk = stats[:, 20:20 + NT]
    assert 20 + 2 * NT <= 56

    assert 2 * TC >= D
    psQ = [nc.alloc_psum_tensor(f"psQ{i}", [128, TC], F32).ap() for i in range(2)]
    psA = nc.alloc_psum_tensor("psA", [128, D], BF16).ap()
    psB = nc.alloc_psum_tensor("psB", [128, TC], F32).ap()
    psBbf = psB.bitcast(BF16)[:, 0:D]
    psA2 = [psA, psQ[0].bitcast(BF16)[:, 0:D]]
    psB2 = [psBbf, psQ[1].bitcast(BF16)[:, 0:D]]

    assert HQ % 2 == 0
    chunks = [("q", c) for c in range(HQ)]
    for j in range(HQ // 2):
        chunks += [("k", 2 * j), ("k", 2 * j + 1), ("v", 2 * j), ("v", 2 * j + 1)]
    NCH = len(chunks)
    def pair_of(n):
        if n < HQ:
            return n // 2
        m_ = n - HQ
        return HQ // 2 + 2 * (m_ // 4) + (m_ % 4) // 2
    pair_col = []
    for p in range(3 * HQ // 2):
        if p < HQ // 2:
            pair_col.append(p * 256)
        else:
            g, r = divmod(p - HQ // 2, 2)
            pair_col.append((HQ * 128) * (1 + r) + g * 256)
    NP = len(pair_col)

    def jcol(kind, c):
        return {"q": 0, "k": HQ, "v": 2 * HQ}[kind] + c

    LA, LB = 2, 6
    order = [("quant", 0)]
    for s in range(FS + LB + 1):
        if s + 1 < FS:
            order.append(("quant", s + 1))
        if LA <= s < FS + LA:
            order.append(("e1", s - LA))
        if s == LB - 1:
            order.append(("g", -1))
        if LB <= s < FS + LB:
            order.append(("e3", s - LB))
        if LB + 1 <= s:
            order.append(("e4", s - LB - 1))
    early_q = {}
    if HQ >= 2:
        p0 = max(0, HQ - 5)
        p1 = max(p0 + 1, HQ - 3)
        early_q = {p0: 0, p1: 1}
    early_mset = set(early_q.values())
    QNAMES = ["wmax", "wmaxc", "ws", "rcp", "winv", "tsw", "wqbf"]

    P = {}
    P["sxa"] = 4 * 16
    P["swr"] = [16 if m < 2 else 16 * ((m - 2) // 2 + 1) for m in range(FS)]
    def sw_val(p):
        if NSTG and p >= 6:
            return 16 * ((p - 6) // 2 + 2)
        return 16 * (p // 2 + 1)
    P["sw"] = [sw_val(pair_of(n)) for n in range(NCH)]
    P["scc_bounce"], P["scc_cc"], P["scc_ret"] = 16, 1, 16
    P["so"] = [16 * (m // 2 + 1) for m in range(FS)]

    v = 0
    v += 1; P["v_ones"] = v
    v += 1; P["v_onec"] = v
    v += 1; P["v_halfc"] = v
    pa_v = [dict() for _ in range(NT)]
    for i in range(NT + 1):
        if i < NT:
            for nm in ["mx", "mn", "m", "m2", "a"]:
                v += 1; pa_v[i][nm] = v
        if i >= 1:
            for nm in ["nmr", "dhi", "dlo", "xmax", "xs", "xsc", "ixs", "tsq", "xqbf",
                       "xnbf"]:
                v += 1; pa_v[i - 1][nm] = v
    P["v_pa_end"] = v
    pf_v = [dict() for _ in range(FS)]
    pb_v = []
    vi = 0
    for (kind, c) in chunks:
        if kind == "v":
            v += 1; pb_v.append(v)
            if vi in early_q:
                for nm in QNAMES:
                    v += 1; pf_v[early_q[vi]][nm] = v
            vi += 1
        else:
            pb_v.append(None)
    P["v_pb_end"] = v
    for (what, m) in order:
        if what == "quant":
            if m in early_mset:
                continue
            for nm in QNAMES:
                v += 1; pf_v[m][nm] = v
        elif what == "g":
            v += 1; P["v_p"] = v
            v += 1; P["v_g"] = v
        else:
            v += 1; pf_v[m][what] = v
    P["v_end"] = v

    a = 0
    pa_a = [dict() for _ in range(NT)]
    pb_a0 = [None] * max(EARLY, 1)
    for i in range(NT + 2):
        if i < NT:
            for nm in ["sum", "ssq"]:
                a += 1; pa_a[i][nm] = a
        if 1 <= i <= NT:
            for nm in ["lnp", "r", "xn"]:
                a += 1; pa_a[i - 1][nm] = a
        if i >= 2:
            for nm in ["evA", "evB"]:
                a += 1; pa_a[i - 2][nm] = a
        if OVL and i >= 7 and (i - 7) < NE3:
            a += 1; pb_a0[i - 7] = a
    for n in range(NE3, EARLY):
        a += 1; pb_a0[n] = a
    a += 1; P["a_xsrow"] = a
    pb_a = []
    for (kind, c) in chunks:
        if kind in ("q", "k"):
            a += 1; ev = a
            a += 1; pb_a.append({"evac": ev, "accum": a})
        else:
            a += 1; pb_a.append({"accum": a})
    P["a_pb_end"] = a
    pf_a = [dict() for _ in range(FS)]
    for (what, m) in order:
        if what == "quant":
            a += 1; pf_a[m]["evWa"] = a
            if SPLIT:
                a += 1; pf_a[m]["evWb"] = a
        elif what == "g":
            a += 1; P["a_lnp"] = a
            a += 1; P["a_rs"] = a
        elif what == "e1":
            a += 1; pf_a[m]["e2"] = a
    P["a_end"] = a

    t = 0
    pa_t = []
    pb_t0 = [None] * max(EARLY, 1)
    for i in range(NT):
        t += 1; g1 = t
        t += 1; pa_t.append((g1, t))
        if OVL and i >= 5 and (i - 5) < NE3:
            t += 1; pb_t0[i - 5] = t
    for n in range(NE3, EARLY):
        t += 1; pb_t0[n] = t
    t += 1; P["t_xsbc"] = t
    pb_t = []
    for _ in chunks:
        t += 1; pb_t.append(t)
    P["t_pb_end"] = t
    pf_t = [dict() for _ in range(FS)]
    torder = []
    for m in range(FS + 8):
        if m < FS:
            torder.append(("twqA", m))
            if SPLIT:
                torder.append(("twqB", m))
        if 1 <= m <= FS:
            torder.append(("mmA", m - 1))
            if SPLIT:
                torder.append(("mmB", m - 1))
        if 7 <= m and (m - 7) < FS:
            torder.append(("tout", m - 7))
    assert sorted(m for (w, m) in torder if w == "tout") == list(range(FS))
    assert sorted(m for (w, m) in torder if w == "mmA") == list(range(FS))
    for (what, m) in torder:
        t += 1; pf_t[m][what] = t
    P["t_end"] = t

    import contextlib
    es = contextlib.ExitStack()
    with es:
        sxa = es.enter_context(nc.semaphore("sxa"))
        sxs = es.enter_context(nc.semaphore("sxs"))
        sxt = [es.enter_context(nc.semaphore(f"sxt{i}")) for i in range(NT)]
        sw2 = [es.enter_context(nc.semaphore(f"sw{i}")) for i in range(2)]
        swr2 = [es.enter_context(nc.semaphore(f"swr{i}")) for i in range(2)]
        swrh2 = [es.enter_context(nc.semaphore(f"swrh{i}")) for i in range(2)]
        sstg = [es.enter_context(nc.semaphore(f"sstg{i}")) for i in range(max(NSTG, 1))]
        sxr2 = [es.enter_context(nc.semaphore(f"sxr{i}")) for i in range(2)]
        so2 = [es.enter_context(nc.semaphore(f"so{i}")) for i in range(2)]
        scb = es.enter_context(nc.semaphore("scb"))
        sccol = es.enter_context(nc.semaphore("sccol"))
        scret = es.enter_context(nc.semaphore("scret"))
        st = es.enter_context(nc.semaphore("st"))
        sv = es.enter_context(nc.semaphore("sv"))
        sa = es.enter_context(nc.semaphore("sa"))
        block = es.enter_context(nc.Block())

        def stepper(eng, self_sem):
            state = {"c": 0}

            def step(emit, *waits):
                for (s, val) in waits:
                    if val and val > 0:
                        eng.wait_ge(s, val)
                if state["c"] > 0:
                    eng.wait_ge(self_sem, state["c"])
                emit().then_inc(self_sem, 1)
                state["c"] += 1
                return state["c"]
            return step, state

        @block.sync
        def _(sync):
            sync.dma_start(out=battn[:, :], in_=bcol_h).then_inc(sxa, 16)
            sync.dma_start(out=biasc[:, :], in_=biascol_h).then_inc(sxa, 16)
            sync.dma_start(out=identb[:, :], in_=identb_h).then_inc(sxa, 16)
            sync.dma_start(out=identf[:, :], in_=identf_h).then_inc(sxa, 16)
            for i in range(NT):
                if i >= 3:
                    sync.wait_ge(sa, pa_a[i - 3]["xn"])
                    sync.wait_ge(sv, pa_v[i - 3]["mn"])
                sync.dma_start(out=xstage[i % 3][:, :],
                               in_=x_h[i * 128:(i + 1) * 128, :]).then_inc(sxt[i], 16)
            sync.wait_ge(sv, pa_v[NT - 1]["ixs"])
            sync.dma_start(out=xs_dram.ap().transpose([1, 0]), in_=xs_blk,
                           allow_slow_non_contiguous=True).then_inc(sxs, 16)
            sync.wait_ge(sxs, 16)
            sync.dma_start(out=xsrow1[:, :],
                           in_=xs_dram.ap().rearrange("j p -> (j p)").unsqueeze(0)).then_inc(sxs, 16)
            i2l = [i for i in range(NT) if i % 3 == 2]
            if i2l:
                sync.wait_ge(sa, pa_a[i2l[-1]]["xn"])
                sync.wait_ge(sv, pa_v[i2l[-1]]["mn"])
            for m in range(min(2, FS)):
                if NSTG and m == 1:
                    sync.wait_ge(st, pb_t[7])
                sync.dma_start(out=wrow[m % 2][:, :],
                               in_=wt_h[m * 128:(m + 1) * 128, :]).then_inc(swrh2[m % 2], 16)

            def xre_dma(m):
                srcap = x_h[:, m * 128:(m + 1) * 128].rearrange("(i p) c -> p i c", p=128)
                dstap = xre[m % 2].rearrange("p (i c) -> p i c", i=NT)
                sync.dma_start(out=dstap, in_=srcap).then_inc(sxr2[m % 2], 16)
            if NSTG:
                sync.wait_ge(st, pb_t[11])
            for m in range(min(2, FS)):
                xre_dma(m)
            sync.wait_ge(sccol, P["scc_cc"])
            sync.dma_start(out=Sglob[:, :], in_=cc_out.ap()).then_inc(scret, 16)
            for m in range(FS):
                sync.wait_ge(sv, pf_v[m]["e4"])
                if m + 2 < FS:
                    xre_dma(m + 2)
                dst = out_h.rearrange("(i p) c -> p i c", p=128)[:, :, m * 128:(m + 1) * 128]
                srcap = outb[m % 2].rearrange("p (i c) -> p i c", i=NT)
                sync.dma_start(out=dst, in_=srcap).then_inc(so2[m % 2], 16)
            sync.wait_ge(so2[0], 16 * ((FS + 1) // 2))
            sync.wait_ge(so2[1], 16 * (FS // 2))
            if debug:
                views = dict(xnT=xnT, xqT=xqT, qT=qT, Sloc=Sloc, Sglob=Sglob,
                             g=g_sb, xsrow=xsrow, wscol=wscol, stats=stats)
                ndbg = 0
                for nm, ap in dbg_h.items():
                    sync.dma_start(out=ap, in_=views[nm]).then_inc(sxs, 16)
                    ndbg += 1
                sync.wait_ge(sxs, 32 + 16 * ndbg)

        @block.gpsimd
        def _(gpsimd):
            def wdma(p):
                o0 = pair_col[p]
                srcap = W_full[:, o0:o0 + 256].rearrange("(k q) c -> q k c", q=128)
                dst = wbuf[p % 2].rearrange("p (k c) -> p k c", k=KC)
                gpsimd.dma_start(out=dst, in_=srcap).then_inc(sw2[p % 2], 16)
            wdma(0)
            wdma(1)

            def stgdma(j):
                o0 = pair_col[2 + j]
                srcap = W_full[:, o0:o0 + 256].rearrange("(k q) c -> q k c", q=128)
                dst = stg[j].rearrange("p (k c) -> p k c", k=KC)
                gpsimd.dma_start(out=dst, in_=srcap).then_inc(sstg[j], 16)
            if NSTG:
                stgdma(0)
                stgdma(1)
                stgdma(3)
                gpsimd.wait_ge(sa, pa_a[6]["xn"])
                gpsimd.wait_ge(sv, pa_v[6]["mn"])
                stgdma(2)
            last_chunk_of_pair = {}
            for n in range(NCH):
                last_chunk_of_pair[pair_of(n)] = n
            for p in range(2 if not NSTG else 6, NP):
                if NSTG and p in (6, 7):
                    gpsimd.wait_ge(st, pb_t[last_chunk_of_pair[p - 6]])
                else:
                    gpsimd.wait_ge(st, pb_t[last_chunk_of_pair[p - 2]])
                wdma(p)
            gpsimd.wait_ge(sv, P["v_pb_end"])
            gpsimd.wait_ge(sa, P["a_pb_end"])
            gpsimd.dma_start(out=cc_in.ap(), in_=Sloc[:, :]).then_inc(scb, 16)
            gpsimd.wait_ge(scb, 16)
            if use_cc:
                gpsimd.collective_compute(
                    "AllReduce", ALU.add, replica_groups=[list(range(n_cores))],
                    ins=[cc_in.ap().opt()], outs=[cc_out.ap().opt()],
                ).then_inc(sccol, 1)
            else:
                gpsimd.dma_start(out=cc_out.ap(), in_=cc_in.ap()).then_inc(sccol, 1)
            for m in range(FS - 2):
                m2 = m + 2
                gpsimd.wait_ge(sv, pf_v[m]["tsw"])
                gpsimd.dma_start(out=wrow[m2 % 2][:, :],
                                 in_=wt_h[m2 * 128:(m2 + 1) * 128, :]).then_inc(swr2[m2 % 2], 16)

        @block.scalar
        def _(scalar):
            step, _sa8 = stepper(scalar, sa)
            scalar.wait_ge(sxa, P["sxa"])
            for i in range(NT + 2):
                if i < NT:
                    xi = xstage[i % 3]
                    step(lambda i=i, xi=xi: scalar.activation(
                            junk2[:, 0:D], xi[:, :], AF.Identity,
                            accum_out=stc(C_SUM, i)),
                         (sxt[i], 16),
                         (sv, pa_v[i - 2]["m"] if i > 1 else 0))
                    step(lambda i=i, xi=xi: scalar.activation(
                            junk2[:, 0:D], xi[:, :], AF.Square,
                            accum_out=stc(C_SSQ, i)),
                         (sv, pa_v[i - 2]["a"] if i > 1 else 0))
                if 1 <= i <= NT:
                    j = i - 1
                    xj = xstage[j % 3]
                    step(lambda j=j: scalar.activation(stc(C_R, j), stc(C_A, j), AF.Ln,
                                                       bias=halfc),
                         (sv, pa_v[j]["a"]),
                         (sv, P["v_halfc"]),
                         (sv, pa_v[j - 2]["nmr"] if j > 1 else 0))
                    step(lambda j=j: scalar.activation(stc(C_R, j), stc(C_R, j),
                                                       AF.Exp, scale=-0.5))
                    step(lambda j=j, xj=xj: scalar.activation(
                            scr0[j % 2][:, :], xj[:, :], AF.Identity,
                            scale=stc(C_R, j), bias=stc(C_NMR, j)),
                         (sv, pa_v[j]["nmr"]),
                         (sv, pa_v[j - 2]["xnbf"] if j > 1 else 0))
                if i >= 2:
                    jj = i - 2
                    pA = psA2[jj % 2] if (not OVL or jj < 4) else psA
                    pB = psB2[jj % 2] if (not OVL or jj < 4) else psBbf
                    step(lambda jj=jj, pA=pA: scalar.activation(
                            xnT.rearrange("p (k t) -> p k t", k=KC)[:, :, jj * 128:(jj + 1) * 128],
                            pA.rearrange("p (k t) -> p k t", k=KC), AF.Identity),
                         (st, pa_t[jj][0]))
                    step(lambda jj=jj, pB=pB: scalar.activation(
                            xqT.rearrange("p (k t) -> p k t", k=KC)[:, :, jj * 128:(jj + 1) * 128],
                            pB.rearrange("p (k t) -> p k t", k=KC), AF.Identity),
                         (st, pa_t[jj][1]))
                if OVL and i >= 7 and (i - 7) < NE3:
                    n0 = i - 7
                    step(lambda n0=n0: scalar.activation(
                            qT[:, n0 * TC:n0 * TC + NH], psQ[n0 % 2][:, 0:NH],
                            AF.Identity, bias=battn[:, n0:n0 + 1], scale=onec),
                         (st, pb_t0[n0]),
                         (sv, P["v_onec"] if n0 == 0 else 0))
            for n0 in range(NE3, EARLY):
                step(lambda n0=n0: scalar.activation(
                        qT[:, n0 * TC:n0 * TC + NH], psQ[n0 % 2][:, 0:NH],
                        AF.Identity, bias=battn[:, n0:n0 + 1], scale=onec),
                     (st, pb_t0[n0]))
            step(lambda: scalar.activation(xsrow[:, :], psB[:, 0:TC], AF.Identity),
                 (st, P["t_xsbc"]))
            last_v_par = {0: 0, 1: 0}
            for n, (kind, c) in enumerate(chunks):
                j = jcol(kind, c)
                if kind == "q":
                    if n < EARLY:
                        step(lambda n=n, c=c, j=j: scalar.activation(
                                qT[:, c * TC + NH:(c + 1) * TC], psQ[n % 2][:, NH:TC],
                                AF.Identity, bias=battn[:, j:j + 1], scale=onec),
                             (st, pb_t[n]))
                    else:
                        step(lambda n=n, c=c, j=j: scalar.activation(
                                qT[:, c * TC:(c + 1) * TC], psQ[n % 2][:, :],
                                AF.Identity, bias=battn[:, j:j + 1], scale=onec),
                             (st, pb_t[n]),
                             (sv, P["v_onec"]))
                    step(lambda c=c: scalar.activation(
                            junk2[:, 0:TC], qT[:, c * TC:(c + 1) * TC], AF.Square,
                            accum_out=Sloc[:, c:c + 1]))
                elif kind == "k":
                    pk = c % 2
                    step(lambda n=n, j=j, pk=pk: scalar.activation(
                            scr1[:, pk * TC:(pk + 1) * TC],
                            psQ[n % 2][:, :], AF.Identity, bias=battn[:, j:j + 1],
                            scale=onec),
                         (st, pb_t[n]),
                         (sv, pb_v[last_v_par[pk]] if last_v_par[pk] else 0),
                         (sv, pa_v[NT - 1]["xqbf"] if last_v_par[pk] == 0 else 0))
                    step(lambda c=c, pk=pk: scalar.activation(
                            junk2[:, 0:TC], scr1[:, pk * TC:(pk + 1) * TC], AF.Square,
                            accum_out=Sloc[:, HQ + c:HQ + c + 1]))
                else:
                    last_v_par[c % 2] = n
                    step(lambda c=c: scalar.activation(
                            junk2[:, 0:TC], junk[:, 0:TC], AF.Identity,
                            accum_out=Sloc[:, 2 * HQ + c:2 * HQ + c + 1]),
                         (sv, pb_v[n]))
            for (what, m) in order:
                if what == "quant":
                    step(lambda m=m: scalar.activation(
                            wqTb[m % 2][:, 0:KH * 128], psA[:, 0:KH * 128],
                            AF.Identity),
                         (st, pf_t[m]["twqA"]),
                         (st, pf_t[m - 2]["mmA"] if m > 1 else 0))
                    if SPLIT:
                        step(lambda m=m: scalar.activation(
                                wqTb[m % 2][:, KH * 128:KC * 128],
                                psA[:, KH * 128:KC * 128], AF.Identity),
                             (st, pf_t[m]["twqB"]),
                             (st, pf_t[m - 2]["mmB"] if m > 1 else 0))
                elif what == "g":
                    step(lambda: scalar.activation(gtmp[:, :], gtmp[:, :], AF.Ln),
                         (sv, P["v_p"]))
                    step(lambda: scalar.activation(gtmp[:, :], gtmp[:, :], AF.Exp,
                                                   scale=-0.5))
                elif what == "e1":
                    step(lambda m=m: scalar.activation(
                            scrU2[m % 4][:, :], scrU[m % 2][:, :], AF.Identity,
                            scale=wscol[:, m:m + 1], bias=biasc[:, m:m + 1]),
                         (sv, pf_v[m]["e1"]),
                         (sv, pf_v[m - 4]["e3"] if m > 3 else 0))
            assert _sa8["c"] == P["a_end"]

        @block.vector
        def _(vector):
            step, _sv8 = stepper(vector, sv)
            vector.wait_ge(sxa, P["sxa"])
            step(lambda: vector.memset(ones_sb[:, :], 1.0))
            step(lambda: vector.memset(onec, 1.0))
            step(lambda: vector.memset(halfc, eps))
            invD = 1.0 / D
            C_MX, C_MN = C_SSQ, C_M2
            for i in range(NT + 1):
                if i < NT:
                    xi = xstage[i % 3]
                    step(lambda i=i, xi=xi: vector.reduce_max(stats[:, 16 + (i & 1):17 + (i & 1)],
                                                              xi[:, :], axis=AX),
                         (sxt[i], 16),
                         (sv, pa_v[i - 2]["dhi"] if i > 1 else 0))
                    step(lambda i=i, xi=xi: vector.tensor_reduce(
                            stats[:, 18 + (i & 1):19 + (i & 1)], xi[:, :],
                            axis=AX, op=ALU.min),
                         (sv, pa_v[i - 2]["dlo"] if i > 1 else 0))
                    step(lambda i=i: vector.tensor_scalar_mul(stc(C_M, i),
                                                              stc(C_SUM, i), invD),
                         (sa, pa_a[i]["sum"]),
                         (sv, pa_v[i - 2]["nmr"] if i > 1 else 0))
                    step(lambda i=i: vector.tensor_mul(stc(C_M2, i), stc(C_M, i),
                                                       stc(C_M, i)))
                    step(lambda i=i: vector.scalar_tensor_tensor(
                            stc(C_A, i), stc(C_SSQ, i), invD, stc(C_M2, i),
                            ALU.mult, ALU.subtract),
                         (sa, pa_a[i]["ssq"]),
                         (sa, pa_a[i - 2]["lnp"] if i > 1 else 0))
                if i >= 1:
                    j = i - 1
                    step(lambda j=j: vector.tensor_scalar(stc(C_NMR, j), stc(C_M, j),
                                                          stc(C_R, j), -1.0,
                                                          ALU.mult, ALU.mult),
                         (sa, pa_a[j]["r"]),
                         (sa, pa_a[j - 2]["xn"] if j > 1 else 0))
                    step(lambda j=j: vector.tensor_sub(
                            stats[:, 16 + (j & 1):17 + (j & 1)],
                            stats[:, 16 + (j & 1):17 + (j & 1)], stc(C_M, j)))
                    step(lambda j=j: vector.tensor_sub(
                            stats[:, 18 + (j & 1):19 + (j & 1)],
                            stc(C_M, j), stats[:, 18 + (j & 1):19 + (j & 1)]))
                    step(lambda j=j: vector.tensor_max(
                            stc(C_XMAX, j),
                            stats[:, 16 + (j & 1):17 + (j & 1)],
                            stats[:, 18 + (j & 1):19 + (j & 1)]))
                    step(lambda j=j: vector.scalar_tensor_tensor(
                            xs_col[j], stc(C_XMAX, j), 1.0 / 127.0, stc(C_R, j),
                            ALU.mult, ALU.mult),
                         (sa, pa_a[j]["r"]))
                    step(lambda j=j: vector.tensor_scalar_max(xs_col[j], xs_col[j],
                                                              1e-12))
                    step(lambda j=j: vector.reciprocal(ixs_col[j], xs_col[j]))
                    step(lambda j=j: vector.tensor_scalar_mul(scr1_i32[:, :],
                                                              scr0[j % 2][:, :],
                                                              ixs_col[j]),
                         (sa, pa_a[j]["xn"]))
                    step(lambda: vector.tensor_copy(xqbf[:, :], scr1_i32[:, :]),
                         (st, pa_t[j - 1][1] if j > 0 else 0))
                    step(lambda j=j: vector.tensor_copy(xnbf[:, :], scr0[j % 2][:, :]),
                         (sa, pa_a[j]["xn"]),
                         (st, pa_t[j - 1][0] if j > 0 else 0))
            assert _sv8["c"] == P["v_pa_end"]

            def emit_quant(m):
                step(lambda m=m: vector.reduce_max(stc(C_XMAX, m), wrow[m % 2][:, :],
                                                   axis=AX, apply_absolute_value=True),
                     ((swrh2 if m < 2 else swr2)[m % 2], P["swr"][m]))
                step(lambda m=m: vector.tensor_scalar_max(stc(C_SUM, m),
                                                          stc(C_XMAX, m), 1e-8))
                step(lambda m=m: vector.tensor_scalar_mul(wscol[:, m:m + 1],
                                                          stc(C_SUM, m), 1.0 / 127.0))
                step(lambda m=m: vector.reciprocal(stc(C_M, m), stc(C_SUM, m)))
                step(lambda m=m: vector.tensor_scalar_mul(winv[:, m:m + 1],
                                                          stc(C_M, m), 127.0))
                step(lambda m=m: vector.tensor_scalar_mul(scr0_i32[:, :],
                                                          wrow[m % 2][:, :],
                                                          winv[:, m:m + 1]))
                step(lambda m=m: vector.tensor_copy(wqbf2[m % 2][:, :], scr0_i32[:, :]),
                     (st, pf_t[m - 2][TWQL] if m > 1 else
                          (pa_t[NT - 1][1] if m == 0 else pa_t[NT - 1][0])))
            prev_v = 0
            vi = 0
            kidx = {}
            for n, (kind, c) in enumerate(chunks):
                if kind == "k":
                    kidx[c] = n
                if kind != "v":
                    continue
                j = jcol(kind, c)
                pk = c % 2
                step(lambda n=n, j=j, pk=pk: vector.scalar_tensor_tensor(
                        junk[:, 0:TC], psQ[n % 2][:, :], battn[:, j:j + 1],
                        scr1[:, pk * TC:(pk + 1) * TC], ALU.add, ALU.mult),
                     (st, pb_t[n]),
                     (sa, pb_a[kidx[c]]["evac"]),
                     (sa, pb_a[prev_v]["accum"] if prev_v else 0))
                prev_v = n
                if vi in early_q:
                    emit_quant(early_q[vi])
                vi += 1
            assert _sv8["c"] == P["v_pb_end"]
            for (what, m) in order:
                if what == "quant":
                    if m not in early_mset:
                        emit_quant(m)
                elif what == "g":
                    step(lambda: vector.tensor_mul(gtmp[:, :], Sglob[:, 0:HQ],
                                                   Sglob[:, HQ:2 * HQ]),
                         (scret, P["scc_ret"]))
                    step(lambda: vector.tensor_mul(g_sb[:, :], Sglob[:, 2 * HQ:3 * HQ],
                                                   gtmp[:, :]),
                         (sa, P["a_rs"]))
                elif what == "e1":
                    step(lambda m=m: vector.tensor_mul(scrU[m % 2][:, :],
                                                       psQ[m % 2][:, :], xsrow[:, :]),
                         (st, pf_t[m][MML]),
                         (sa, P["a_xsrow"] if m == 0 else 0),
                         (sa, pf_a[m - 2]["e2"] if m > 1 else 0))
                elif what == "e3":
                    step(lambda m=m: vector.scalar_tensor_tensor(
                            scrO[m % 2][:, :], qT[:, m * TC:(m + 1) * TC],
                            g_sb[:, m:m + 1],
                            scrU2[m % 4][:, :], ALU.mult, ALU.add),
                         (sa, pf_a[m]["e2"]),
                         (st, pf_t[m - 2]["tout"] if m > 1 else 0))
                else:
                    step(lambda m=m: vector.tensor_tensor(
                            outb[m % 2][:, :], psB[:, 0:TC], xre[m % 2][:, :],
                            ALU.add),
                         (st, pf_t[m]["tout"]),
                         (sxr2[m % 2], 16 * (m // 2 + 1)),
                         (so2[m % 2], P["so"][m - 2] if m > 1 else 0))
            assert _sv8["c"] == P["v_end"]

        @block.tensor
        def _(tensor):
            tcnt = {"c": 0}

            def group(emits, *waits):
                for (s, val) in waits:
                    if val and val > 0:
                        tensor.wait_ge(s, val)
                if tcnt["c"] > 0:
                    tensor.wait_ge(st, tcnt["c"])
                last = None
                for e in emits:
                    last = e()
                last.then_inc(st, 1)
                tcnt["c"] += 1
                return tcnt["c"]

            def lhsT_of(n, k):
                p = pair_of(n)
                sub = (jcol(chunks[n][0], chunks[n][1]) * 128 - pair_col[p]) // 128
                if NSTG and 4 <= n < EARLY:
                    buf = stg[p - 2]
                else:
                    buf = wbuf[p % 2]
                return buf.rearrange("p (k c) -> p k c", k=KC)[:, k, sub * 128:(sub + 1) * 128]

            def emit_h0(n):
                p = pair_of(n)
                if NSTG and 4 <= n < EARLY:
                    waits = [(sstg[p - 2], 16)]
                else:
                    waits = [(sw2[p % 2], P["sw"][n])]
                if n >= 2:
                    waits.append((sa, pb_a0[n - 2]))
                elif n == 0:
                    waits.append((sa, pa_a[3]["evA"]))
                else:
                    waits.append((sa, pa_a[3]["evB"]))
                group([lambda n=n, k=k: tensor.matmul(
                        psQ[n % 2][:, 0:NH], lhsT_of(n, k),
                        xnT[:, k * TC: k * TC + NH],
                        start=(k == 0), stop=(k == KC - 1)) for k in range(KC)],
                      *waits)

            for i in range(NT):
                pA = psA2[i % 2] if (not OVL or i < 4) else psA
                pB = psB2[i % 2] if (not OVL or i < 4) else psBbf
                if not OVL or i < 4:
                    wA = (sa, pa_a[i - 2]["evA"] if i > 1 else 0)
                    wB = (sa, pa_a[i - 2]["evB"] if i > 1 else 0)
                else:
                    wA = (sa, pa_a[i - 1 if i > 4 else 2]["evA"])
                    wB = (sa, pa_a[i - 1 if i > 4 else 2]["evB"])
                group([lambda k=k, pA=pA: tensor.transpose(
                            pA[:, k * 128:(k + 1) * 128],
                            xnbf[:, k * 128:(k + 1) * 128],
                            identb[:, :]) for k in range(KC)],
                      (sxa, P["sxa"]),
                      (sv, pa_v[i]["xnbf"]),
                      wA)
                group([lambda k=k, pB=pB: tensor.transpose(
                            pB[:, k * 128:(k + 1) * 128],
                            xqbf[:, k * 128:(k + 1) * 128],
                            identb[:, :]) for k in range(KC)],
                      (sv, pa_v[i]["xqbf"]),
                      wB)
                if OVL and i >= 5 and (i - 5) < NE3:
                    emit_h0(i - 5)
            for n in range(NE3, EARLY):
                emit_h0(n)
            group([(lambda h=h: tensor.matmul(psB[:, h * NH:(h + 1) * NH], ones_sb[:, :],
                                              xsrow1[:, h * NH:(h + 1) * NH],
                                              start=True, stop=True)) for h in range(HL)],
                  (sxs, 32),
                  (sv, P["v_ones"]),
                  (sa, pa_a[NT - 1]["evB"]))
            for n, (kind, c) in enumerate(chunks):
                waits = []
                if n < EARLY:
                    if n == 0:
                        waits.append((sa, pa_a[NT - 1]["evA"]))
                    if n >= 2:
                        waits.append((sa, pb_a[n - 2]["evac"]))
                    group([lambda n=n, k=k: tensor.matmul(
                            psQ[n % 2][:, NH:TC], lhsT_of(n, k),
                            xnT[:, k * TC + NH: k * TC + TC],
                            start=(k == 0), stop=(k == KC - 1)) for k in range(KC)],
                          *waits)
                    continue
                waits = [(sw2[pair_of(n) % 2], P["sw"][n])]
                if n == 0:
                    waits.append((sa, pa_a[NT - 1]["evB"]))
                if n >= 2:
                    k2, _ = chunks[n - 2]
                    if k2 == "v":
                        waits.append((sv, pb_v[n - 2]))
                    else:
                        waits.append((sa, pb_a[n - 2]["evac"]))
                emits = []
                for k in range(KC):
                    for h in range(HL):
                        emits.append(lambda n=n, k=k, h=h: tensor.matmul(
                            psQ[n % 2][:, h * NH:(h + 1) * NH], lhsT_of(n, k),
                            xnT[:, k * TC + h * NH: k * TC + (h + 1) * NH],
                            start=(k == 0), stop=(k == KC - 1)))
                group(emits, *waits)
            for (what, m) in torder:
                if what == "twqA":
                    waits = [(sv, pf_v[m]["wqbf"]),
                             (sa, pf_a[m - 1]["evWa"] if m > 0 else 0)]
                    if m == 0:
                        waits.append((sa, pa_a[NT - 1]["evA"]))
                    group([lambda k=k, m=m: tensor.transpose(
                            psA[:, k * 128:(k + 1) * 128],
                            wqbf2[m % 2][:, k * 128:(k + 1) * 128],
                            identb[:, :]) for k in range(KH)],
                          *waits)
                elif what == "twqB":
                    group([lambda k=k, m=m: tensor.transpose(
                            psA[:, k * 128:(k + 1) * 128],
                            wqbf2[m % 2][:, k * 128:(k + 1) * 128],
                            identb[:, :]) for k in range(KH, KC)],
                          (sa, pf_a[m - 1]["evWb"] if m > 0 else 0))
                elif what in ("mmA", "mmB"):
                    ka, kb = (0, KH) if what == "mmA" else (KH, KC)
                    if what == "mmA":
                        waits = [(sa, pf_a[m]["evWa"])]
                        if m >= 2:
                            waits.append((sv, pf_v[m - 2]["e1"]))
                        else:
                            waits.append((sv, pb_v[NCH - 2 + m]))
                    else:
                        waits = [(sa, pf_a[m]["evWb"])]
                    emits = []
                    for k in range(ka, kb):
                        lhsT = wqTb[m % 2].rearrange("p (k c) -> p k c", k=KC)[:, k, :]
                        for h in range(HL):
                            emits.append(lambda m=m, k=k, h=h, lhsT=lhsT: tensor.matmul(
                                psQ[m % 2][:, h * NH:(h + 1) * NH], lhsT,
                                xqT[:, k * TC + h * NH: k * TC + (h + 1) * NH],
                                start=(k == 0), stop=(k == KC - 1)))
                    group(emits, *waits)
                else:
                    waits = [(sv, pf_v[m]["e3"])]
                    if m == 0:
                        waits.append((sa, P["a_xsrow"]))
                    else:
                        waits.append((sv, pf_v[m - 1]["e4"]))
                    group([lambda i=i: tensor.transpose(
                            psB[:, i * 128:(i + 1) * 128],
                            scrO[m % 2][:, i * 128:(i + 1) * 128],
                            identf[:, :]) for i in range(NT)],
                          *waits)
            assert tcnt["c"] == P["t_end"]

    meta = dict(D=D, TC=TC, NT=NT, KC=KC, HQ=HQ, OC=OC, FS=FS)
    return nc, meta


def _prep_inputs(x, W, b, wt, bias, NT, KC, HQ, n_cores=N_CORES):
    import ml_dtypes
    D = 128 * KC
    TC = 128 * NT
    OC = 3 * HQ
    FS = HQ
    x2 = np.ascontiguousarray(np.asarray(x, dtype=np.float32).reshape(n_cores * TC, D))
    W = np.ascontiguousarray(np.asarray(W, dtype=np.float32))
    wt = np.ascontiguousarray(np.asarray(wt, dtype=np.float32))
    bcol = np.ascontiguousarray(np.asarray(b, dtype=np.float32).reshape(OC, 128).T)
    biascol = np.ascontiguousarray(np.asarray(bias, dtype=np.float32).reshape(FS, 128).T)
    identb = np.ascontiguousarray(np.eye(128, dtype=ml_dtypes.bfloat16))
    identf = np.ascontiguousarray(np.eye(128, dtype=np.float32))
    in_maps = []
    for cid in range(n_cores):
        in_maps.append({
            "x": x2[cid * TC:(cid + 1) * TC],
            "W": W,
            "wt": wt, "bcol": bcol, "biascol": biascol,
            "identb": identb, "identf": identf,
        })
    return in_maps


def kernel(L0_attn_x, L0_attn_W, L0_attn_b, L0_ffn_weight, L0_ffn_bias):
    NT, KC, HQ = 8, 16, 16
    nc, meta = build_nc(NT, KC, HQ)
    in_maps = _prep_inputs(L0_attn_x, L0_attn_W, L0_attn_b,
                           L0_ffn_weight, L0_ffn_bias, NT, KC, HQ)
    res = run_bass_kernel_spmd(nc, in_maps, core_ids=list(range(N_CORES)))
    out = np.concatenate([np.asarray(r["out"]) for r in res.results], axis=0)
    return out.reshape(1, 8192, 2048).astype(np.float32)


# version 21
# speedup vs baseline: 1.0284x; 1.0003x over previous
import numpy as np
import concourse.bass as bass
import concourse.mybir as mybir
from concourse.bass_utils import run_bass_kernel_spmd

F32 = mybir.dt.float32
BF16 = mybir.dt.bfloat16
I32 = mybir.dt.int32
U8 = mybir.dt.uint8
AF = mybir.ActivationFunctionType
ALU = mybir.AluOpType
AX = mybir.AxisListType.X

N_CORES = 8


def build_nc(NT, KC, HQ, n_cores=N_CORES, use_cc=True, debug=False):
    D = 128 * KC
    TC = 128 * NT
    HL = 1 if TC <= 512 else 2
    NH = TC // HL
    assert NH <= 512
    OC = 3 * HQ
    FS = HQ
    SPLIT = (KC // 2) % 8 == 0
    KH = KC // 2 if SPLIT else KC
    TWQL = "twqB" if SPLIT else "twqA"
    MML = "mmB" if SPLIT else "mmA"

    OVL = (HL == 2 and NT == 8)
    EARLY = (12 if HQ >= 16 else min(4, HQ)) if OVL else 0
    NSTG = max(0, (EARLY - 4) // 2)
    NE3 = min(3, EARLY)

    assert FS >= 2
    eps = 1e-5

    nc = bass.Bass("TRN2", target_bir_lowering=False)

    x_h = nc.dram_tensor("x", [TC, D], F32, kind="ExternalInput").ap()
    W_full = nc.dram_tensor("W", [D, 3 * D], F32, kind="ExternalInput").ap()
    wt_h = nc.dram_tensor("wt", [D, D], F32, kind="ExternalInput").ap()
    bcol_h = nc.dram_tensor("bcol", [128, OC], F32, kind="ExternalInput").ap()
    biascol_h = nc.dram_tensor("biascol", [128, FS], F32, kind="ExternalInput").ap()
    identb_h = nc.dram_tensor("identb", [128, 128], BF16, kind="ExternalInput").ap()
    identf_h = nc.dram_tensor("identf", [128, 128], F32, kind="ExternalInput").ap()
    out_h = nc.dram_tensor("out", [TC, D], F32, kind="ExternalOutput").ap()
    dbg_h = {}
    if debug:
        for nm, rows, nel, dt in [("xnT", 128, KC * TC, BF16), ("xqT", 128, KC * TC, BF16),
                                  ("qT", 128, HQ * TC, BF16), ("Sloc", 128, 3 * HQ, F32),
                                  ("Sglob", 128, 3 * HQ, F32), ("g", 128, HQ, F32),
                                  ("xsrow", 128, TC, F32), ("wscol", 128, HQ, F32),
                                  ("stats", 128, 64, F32)]:
            dbg_h[nm] = nc.dram_tensor("dbg_" + nm, [rows, nel], dt, kind="ExternalOutput").ap()
    cc_in = nc.dram_tensor("cc_in", [128, 3 * HQ], F32)
    cc_out = nc.dram_tensor("cc_out", [128, 3 * HQ], F32)

    xs_dram = nc.dram_tensor("xs_dram", [NT, 128], F32)

    off = {}
    cur = 0

    def place(name, nbytes, align=32):
        nonlocal cur
        cur = (cur + align - 1) // align * align
        off[name] = cur
        cur += nbytes

    place("xstage", 2 * D * 4)
    place("xre", 2 * NT * 128 * 4)
    place("xnT", KC * TC * 2)
    place("xqT", KC * TC * 2)
    place("qT", HQ * TC * 2)
    place("scr0", 2 * D * 4)
    place("scr1", max(D, 2 * TC) * 4)
    place("junk", TC * 4)
    place("junk2", max(D, TC) * 4)
    place("xnbf", D * 2)
    place("xqbf", D * 2)
    place("wbuf", 2 * KC * 256 * 2)
    place("wrow", 2 * D * 4)
    place("xsrow", TC * 4)
    place("stats", 64 * 4)
    place("Sloc", 3 * HQ * 4)
    place("Sglob", 3 * HQ * 4)
    place("gtmp", HQ * 4)
    place("g", HQ * 4)
    place("wscol", HQ * 4)
    place("winv", HQ * 4)
    place("battn", OC * 4)
    place("biasc", FS * 4)
    place("identb", 128 * 2)
    place("identf", 128 * 4)
    place("ones", 128 * 4)
    if 2 * KC * 128 * 2 + 2 * TC * 4 + 2 * TC * 4 <= KC * TC * 2:
        off["wqTb"] = off["xnT"]
        off["outb"] = off["wqTb"] + 2 * KC * 128 * 2
        off["scrO"] = off["outb"] + 2 * TC * 4
    else:
        place("wqTb", 2 * KC * 128 * 2)
        place("outb", 2 * TC * 4)
        place("scrO", 2 * TC * 4)
    if 4 * TC * 4 <= 2 * KC * 256 * 2:
        off["scrU"] = off["wbuf"]
        off["scrU2"] = off["wbuf"] + 2 * TC * 4
    else:
        place("scrU", 2 * TC * 4)
        place("scrU2", 2 * TC * 4)
    off["scrU2hi"] = off["scr1"]
    off["xsrow1"] = off["xsrow"]
    arena_bytes = (cur + 31) // 32 * 32
    assert arena_bytes <= 206 * 1024, f"sbuf over budget: {arena_bytes}"

    arena = nc.alloc_sbuf_tensor("arena", [128, arena_bytes], U8).ap()
    ESZ = {F32: 4, BF16: 2, I32: 4}

    def view(name, dt, nelem, rows=128, byte_off=0):
        o = off[name] + byte_off
        return arena[0:rows, o:o + nelem * ESZ[dt]].bitcast(dt)

    xstage = [view("xstage", F32, D, byte_off=0),
              view("xstage", F32, D, byte_off=D * 4),
              view("wrow", F32, D, byte_off=0)]
    xre = [view("xre", F32, NT * 128, byte_off=i * NT * 128 * 4) for i in range(2)]
    xnT = view("xnT", BF16, KC * TC)
    xqT = view("xqT", BF16, KC * TC)
    qT = view("qT", BF16, HQ * TC)
    scr0 = [view("scr0", F32, D, byte_off=i * D * 4) for i in range(2)]
    scr0_i32 = view("scr0", I32, D)
    scr1 = view("scr1", F32, max(D, 2 * TC))
    junk = view("junk", F32, TC)
    junk2 = view("junk2", F32, max(D, TC))
    scr1_i32 = view("scr1", I32, D)
    xnbf = view("xnbf", BF16, D)
    xqbf = view("xqbf", BF16, D)
    wqbf2 = [xqbf, xnbf]
    wbuf = [view("wbuf", BF16, KC * 256, byte_off=i * KC * 256 * 2) for i in range(2)]
    wrow = [view("wrow", F32, D, byte_off=i * D * 4) for i in range(2)]
    wqTb = [view("wqTb", BF16, KC * 128, byte_off=i * KC * 128 * 2) for i in range(2)]
    if NSTG:
        assert NSTG == 4 and (12 + 4) * TC * 2 <= HQ * TC * 2
        assert KC * 256 * 2 <= min(D * 4, 2 * NT * 128 * 4)
        stg = [view("qT", BF16, KC * 256, byte_off=12 * TC * 2),
               view("wrow", BF16, KC * 256, byte_off=D * 4),
               view("xstage", BF16, KC * 256, byte_off=0),
               view("xre", BF16, KC * 256, byte_off=0)]
    else:
        stg = None
    scrU = [view("scrU", F32, TC, byte_off=i * TC * 4) for i in range(2)]
    scrU2 = [view("scrU2", F32, TC, byte_off=i * TC * 4) for i in range(2)] + \
            [view("scrU2hi", F32, TC, byte_off=i * TC * 4) for i in range(2)]
    scrO = [view("scrO", F32, TC, byte_off=i * TC * 4) for i in range(2)]
    outb = [view("outb", F32, TC, byte_off=i * TC * 4) for i in range(2)]
    xsrow = view("xsrow", F32, TC)
    xsrow1 = view("xsrow1", F32, TC, rows=1)
    stats = view("stats", F32, 64)
    Sloc = view("Sloc", F32, 3 * HQ)
    Sglob = view("Sglob", F32, 3 * HQ)
    gtmp = view("gtmp", F32, HQ)
    g_sb = view("g", F32, HQ)
    wscol = view("wscol", F32, HQ)
    winv = view("winv", F32, HQ)
    battn = view("battn", F32, OC)
    biasc = view("biasc", F32, FS)
    identb = view("identb", BF16, 128)
    identf = view("identf", F32, 128)
    ones_sb = view("ones", F32, 128, rows=1)

    C_SUM, C_SSQ, C_M, C_M2, C_A, C_R, C_NMR, C_XMAX = range(8)

    def stc(c, i=0):
        o = 2 * c + (i & 1)
        return stats[:, o:o + 1]

    onec = stats[:, 56:57]
    halfc = stats[:, 57:58]

    xs_col = [stats[:, 20 + i:21 + i] for i in range(NT)]
    ixs_col = [stats[:, 20 + NT + i:21 + NT + i] for i in range(NT)]
    xs_blk = stats[:, 20:20 + NT]
    assert 20 + 2 * NT <= 56

    assert 2 * TC >= D
    psQ = [nc.alloc_psum_tensor(f"psQ{i}", [128, TC], F32).ap() for i in range(2)]
    psA = nc.alloc_psum_tensor("psA", [128, D], BF16).ap()
    psB = nc.alloc_psum_tensor("psB", [128, TC], F32).ap()
    psBbf = psB.bitcast(BF16)[:, 0:D]
    psA2 = [psA, psQ[0].bitcast(BF16)[:, 0:D]]
    psB2 = [psBbf, psQ[1].bitcast(BF16)[:, 0:D]]

    assert HQ % 2 == 0
    chunks = [("q", c) for c in range(HQ)]
    for j in range(HQ // 2):
        chunks += [("k", 2 * j), ("k", 2 * j + 1), ("v", 2 * j), ("v", 2 * j + 1)]
    NCH = len(chunks)
    def pair_of(n):
        if n < HQ:
            return n // 2
        m_ = n - HQ
        return HQ // 2 + 2 * (m_ // 4) + (m_ % 4) // 2
    pair_col = []
    for p in range(3 * HQ // 2):
        if p < HQ // 2:
            pair_col.append(p * 256)
        else:
            g, r = divmod(p - HQ // 2, 2)
            pair_col.append((HQ * 128) * (1 + r) + g * 256)
    NP = len(pair_col)

    def jcol(kind, c):
        return {"q": 0, "k": HQ, "v": 2 * HQ}[kind] + c

    LA, LB = 2, 6
    order = [("quant", 0)]
    for s in range(FS + LB + 1):
        if s + 1 < FS:
            order.append(("quant", s + 1))
        if LA <= s < FS + LA:
            order.append(("e1", s - LA))
        if s == LB - 1:
            order.append(("g", -1))
        if LB <= s < FS + LB:
            order.append(("e3", s - LB))
        if LB + 1 <= s:
            order.append(("e4", s - LB - 1))
    early_q = {}
    if HQ >= 2:
        p0 = max(0, HQ - 5)
        p1 = max(p0 + 1, HQ - 3)
        early_q = {p0: 0, p1: 1}
    early_mset = set(early_q.values())
    QNAMES = ["wmax", "wmaxc", "ws", "rcp", "winv", "tsw", "wqbf"]

    P = {}
    P["sxa"] = 4 * 16
    P["swr"] = [16 if m < 2 else 16 * ((m - 2) // 2 + 1) for m in range(FS)]
    def sw_val(p):
        if NSTG and p >= 6:
            return 16 * ((p - 6) // 2 + 2)
        return 16 * (p // 2 + 1)
    P["sw"] = [sw_val(pair_of(n)) for n in range(NCH)]
    P["scc_bounce"], P["scc_cc"], P["scc_ret"] = 16, 1, 16
    P["so"] = [16 * (m // 2 + 1) for m in range(FS)]

    v = 0
    v += 1; P["v_ones"] = v
    v += 1; P["v_onec"] = v
    v += 1; P["v_halfc"] = v
    pa_v = [dict() for _ in range(NT)]
    for i in range(NT + 1):
        if i < NT:
            for nm in ["mx", "mn", "m", "m2", "a"]:
                v += 1; pa_v[i][nm] = v
        if i >= 1:
            for nm in ["nmr", "dhi", "dlo", "xmax", "xs", "xsc", "ixs", "tsq", "xqbf",
                       "xnbf"]:
                v += 1; pa_v[i - 1][nm] = v
    P["v_pa_end"] = v
    pf_v = [dict() for _ in range(FS)]
    pb_v = []
    vi = 0
    for (kind, c) in chunks:
        if kind == "v":
            v += 1; pb_v.append(v)
            if vi in early_q:
                for nm in QNAMES:
                    v += 1; pf_v[early_q[vi]][nm] = v
            vi += 1
        else:
            pb_v.append(None)
    P["v_pb_end"] = v
    for (what, m) in order:
        if what == "quant":
            if m in early_mset:
                continue
            for nm in QNAMES:
                v += 1; pf_v[m][nm] = v
        elif what == "g":
            v += 1; P["v_p"] = v
            v += 1; P["v_g"] = v
        else:
            v += 1; pf_v[m][what] = v
    P["v_end"] = v

    a = 0
    pa_a = [dict() for _ in range(NT)]
    pb_a0 = [None] * max(EARLY, 1)
    for i in range(NT + 2):
        if i < NT:
            for nm in ["sum", "ssq"]:
                a += 1; pa_a[i][nm] = a
        if 1 <= i <= NT:
            for nm in ["lnp", "r", "xn"]:
                a += 1; pa_a[i - 1][nm] = a
        if i >= 2:
            for nm in ["evA", "evB"]:
                a += 1; pa_a[i - 2][nm] = a
        if OVL and i >= 7 and (i - 7) < NE3:
            a += 1; pb_a0[i - 7] = a
    for n in range(NE3, EARLY):
        a += 1; pb_a0[n] = a
    a += 1; P["a_xsrow"] = a
    pb_a = []
    for (kind, c) in chunks:
        if kind in ("q", "k"):
            a += 1; ev = a
            a += 1; pb_a.append({"evac": ev, "accum": a})
        else:
            a += 1; pb_a.append({"accum": a})
    P["a_pb_end"] = a
    pf_a = [dict() for _ in range(FS)]
    for (what, m) in order:
        if what == "quant":
            a += 1; pf_a[m]["evWa"] = a
            if SPLIT:
                a += 1; pf_a[m]["evWb"] = a
        elif what == "g":
            a += 1; P["a_lnp"] = a
            a += 1; P["a_rs"] = a
        elif what == "e1":
            a += 1; pf_a[m]["e2"] = a
    P["a_end"] = a

    t = 0
    pa_t = []
    pb_t0 = [None] * max(EARLY, 1)
    for i in range(NT):
        t += 1; g1 = t
        t += 1; pa_t.append((g1, t))
        if OVL and i >= 5 and (i - 5) < NE3:
            t += 1; pb_t0[i - 5] = t
    for n in range(NE3, EARLY):
        t += 1; pb_t0[n] = t
    t += 1; P["t_xsbc"] = t
    pb_t = []
    for _ in chunks:
        t += 1; pb_t.append(t)
    P["t_pb_end"] = t
    pf_t = [dict() for _ in range(FS)]
    torder = []
    for m in range(FS + 8):
        if m < FS:
            torder.append(("twqA", m))
            if SPLIT:
                torder.append(("twqB", m))
        if 1 <= m <= FS:
            torder.append(("mmA", m - 1))
            if SPLIT:
                torder.append(("mmB", m - 1))
        if 7 <= m and (m - 7) < FS:
            torder.append(("tout", m - 7))
    assert sorted(m for (w, m) in torder if w == "tout") == list(range(FS))
    assert sorted(m for (w, m) in torder if w == "mmA") == list(range(FS))
    for (what, m) in torder:
        t += 1; pf_t[m][what] = t
    P["t_end"] = t

    import contextlib
    es = contextlib.ExitStack()
    with es:
        sxa = es.enter_context(nc.semaphore("sxa"))
        sxs = es.enter_context(nc.semaphore("sxs"))
        sxt = [es.enter_context(nc.semaphore(f"sxt{i}")) for i in range(NT)]
        sw2 = [es.enter_context(nc.semaphore(f"sw{i}")) for i in range(2)]
        swr2 = [es.enter_context(nc.semaphore(f"swr{i}")) for i in range(2)]
        swrh2 = [es.enter_context(nc.semaphore(f"swrh{i}")) for i in range(2)]
        sstg = [es.enter_context(nc.semaphore(f"sstg{i}")) for i in range(max(NSTG, 1))]
        sxr2 = [es.enter_context(nc.semaphore(f"sxr{i}")) for i in range(2)]
        so2 = [es.enter_context(nc.semaphore(f"so{i}")) for i in range(2)]
        scb = es.enter_context(nc.semaphore("scb"))
        sccol = es.enter_context(nc.semaphore("sccol"))
        scret = es.enter_context(nc.semaphore("scret"))
        st = es.enter_context(nc.semaphore("st"))
        sv = es.enter_context(nc.semaphore("sv"))
        sa = es.enter_context(nc.semaphore("sa"))
        block = es.enter_context(nc.Block())

        def stepper(eng, self_sem):
            state = {"c": 0}

            def step(emit, *waits):
                for (s, val) in waits:
                    if val and val > 0:
                        eng.wait_ge(s, val)
                if state["c"] > 0:
                    eng.wait_ge(self_sem, state["c"])
                emit().then_inc(self_sem, 1)
                state["c"] += 1
                return state["c"]
            return step, state

        @block.sync
        def _(sync):
            sync.dma_start(out=battn[:, :], in_=bcol_h).then_inc(sxa, 16)
            sync.dma_start(out=biasc[:, :], in_=biascol_h).then_inc(sxa, 16)
            sync.dma_start(out=identb[:, :], in_=identb_h).then_inc(sxa, 16)
            sync.dma_start(out=identf[:, :], in_=identf_h).then_inc(sxa, 16)
            for i in range(NT):
                if i >= 3:
                    sync.wait_ge(sa, pa_a[i - 3]["xn"])
                    sync.wait_ge(sv, pa_v[i - 3]["mn"])
                sync.dma_start(out=xstage[i % 3][:, :],
                               in_=x_h[i * 128:(i + 1) * 128, :]).then_inc(sxt[i], 16)
            sync.wait_ge(sv, pa_v[NT - 1]["ixs"])
            sync.dma_start(out=xs_dram.ap().transpose([1, 0]), in_=xs_blk,
                           allow_slow_non_contiguous=True).then_inc(sxs, 16)
            sync.wait_ge(sxs, 16)
            sync.dma_start(out=xsrow1[:, :],
                           in_=xs_dram.ap().rearrange("j p -> (j p)").unsqueeze(0)).then_inc(sxs, 16)
            i2l = [i for i in range(NT) if i % 3 == 2]
            if i2l:
                sync.wait_ge(sa, pa_a[i2l[-1]]["xn"])
                sync.wait_ge(sv, pa_v[i2l[-1]]["mn"])
            for m in range(min(2, FS)):
                if NSTG and m == 1:
                    sync.wait_ge(st, pb_t[7])
                sync.dma_start(out=wrow[m % 2][:, :],
                               in_=wt_h[m * 128:(m + 1) * 128, :]).then_inc(swrh2[m % 2], 16)

            def xre_dma(m):
                srcap = x_h[:, m * 128:(m + 1) * 128].rearrange("(i p) c -> p i c", p=128)
                dstap = xre[m % 2].rearrange("p (i c) -> p i c", i=NT)
                sync.dma_start(out=dstap, in_=srcap).then_inc(sxr2[m % 2], 16)
            if NSTG:
                sync.wait_ge(st, pb_t[11])
            for m in range(min(2, FS)):
                xre_dma(m)
            sync.wait_ge(sccol, P["scc_cc"])
            sync.dma_start(out=Sglob[:, :], in_=cc_out.ap()).then_inc(scret, 16)
            for m in range(FS):
                sync.wait_ge(sv, pf_v[m]["e4"])
                if m + 2 < FS:
                    xre_dma(m + 2)
                dst = out_h.rearrange("(i p) c -> p i c", p=128)[:, :, m * 128:(m + 1) * 128]
                srcap = outb[m % 2].rearrange("p (i c) -> p i c", i=NT)
                sync.dma_start(out=dst, in_=srcap).then_inc(so2[m % 2], 16)
            sync.wait_ge(so2[0], 16 * ((FS + 1) // 2))
            sync.wait_ge(so2[1], 16 * (FS // 2))
            if debug:
                views = dict(xnT=xnT, xqT=xqT, qT=qT, Sloc=Sloc, Sglob=Sglob,
                             g=g_sb, xsrow=xsrow, wscol=wscol, stats=stats)
                ndbg = 0
                for nm, ap in dbg_h.items():
                    sync.dma_start(out=ap, in_=views[nm]).then_inc(sxs, 16)
                    ndbg += 1
                sync.wait_ge(sxs, 32 + 16 * ndbg)

        @block.gpsimd
        def _(gpsimd):
            def wdma(p):
                o0 = pair_col[p]
                srcap = W_full[:, o0:o0 + 256].rearrange("(k q) c -> q k c", q=128)
                dst = wbuf[p % 2].rearrange("p (k c) -> p k c", k=KC)
                gpsimd.dma_start(out=dst, in_=srcap).then_inc(sw2[p % 2], 16)
            wdma(0)
            wdma(1)

            def stgdma(j):
                o0 = pair_col[2 + j]
                srcap = W_full[:, o0:o0 + 256].rearrange("(k q) c -> q k c", q=128)
                dst = stg[j].rearrange("p (k c) -> p k c", k=KC)
                gpsimd.dma_start(out=dst, in_=srcap).then_inc(sstg[j], 16)
            if NSTG:
                stgdma(0)
                stgdma(1)
                stgdma(3)
                gpsimd.wait_ge(sa, pa_a[6]["xn"])
                gpsimd.wait_ge(sv, pa_v[6]["mn"])
                stgdma(2)
            last_chunk_of_pair = {}
            for n in range(NCH):
                last_chunk_of_pair[pair_of(n)] = n
            for p in range(2 if not NSTG else 6, NP):
                if NSTG and p in (6, 7):
                    gpsimd.wait_ge(st, pb_t[last_chunk_of_pair[p - 6]])
                else:
                    gpsimd.wait_ge(st, pb_t[last_chunk_of_pair[p - 2]])
                wdma(p)
            gpsimd.wait_ge(sv, P["v_pb_end"])
            gpsimd.wait_ge(sa, P["a_pb_end"])
            gpsimd.dma_start(out=cc_in.ap(), in_=Sloc[:, :]).then_inc(scb, 16)
            gpsimd.wait_ge(scb, 16)
            if use_cc:
                gpsimd.collective_compute(
                    "AllReduce", ALU.add, replica_groups=[list(range(n_cores))],
                    ins=[cc_in.ap().opt()], outs=[cc_out.ap().opt()],
                ).then_inc(sccol, 1)
            else:
                gpsimd.dma_start(out=cc_out.ap(), in_=cc_in.ap()).then_inc(sccol, 1)
            for m in range(FS - 2):
                m2 = m + 2
                gpsimd.wait_ge(sv, pf_v[m]["tsw"])
                gpsimd.dma_start(out=wrow[m2 % 2][:, :],
                                 in_=wt_h[m2 * 128:(m2 + 1) * 128, :]).then_inc(swr2[m2 % 2], 16)

        @block.scalar
        def _(scalar):
            step, _sa8 = stepper(scalar, sa)
            scalar.wait_ge(sxa, P["sxa"])
            for i in range(NT + 2):
                if i < NT:
                    xi = xstage[i % 3]
                    step(lambda i=i, xi=xi: scalar.activation(
                            junk2[:, 0:D], xi[:, :], AF.Identity,
                            accum_out=stc(C_SUM, i)),
                         (sxt[i], 16),
                         (sv, pa_v[i - 2]["m"] if i > 1 else 0))
                    step(lambda i=i, xi=xi: scalar.activation(
                            junk2[:, 0:D], xi[:, :], AF.Square,
                            accum_out=stc(C_SSQ, i)),
                         (sv, pa_v[i - 2]["a"] if i > 1 else 0))
                if 1 <= i <= NT:
                    j = i - 1
                    xj = xstage[j % 3]
                    step(lambda j=j: scalar.activation(stc(C_R, j), stc(C_A, j), AF.Ln,
                                                       bias=halfc),
                         (sv, pa_v[j]["a"]),
                         (sv, P["v_halfc"]),
                         (sv, pa_v[j - 2]["nmr"] if j > 1 else 0))
                    step(lambda j=j: scalar.activation(stc(C_R, j), stc(C_R, j),
                                                       AF.Exp, scale=-0.5))
                    step(lambda j=j, xj=xj: scalar.activation(
                            scr0[j % 2][:, :], xj[:, :], AF.Identity,
                            scale=stc(C_R, j), bias=stc(C_NMR, j)),
                         (sv, pa_v[j]["nmr"]),
                         (sv, pa_v[j - 2]["xnbf"] if j > 1 else 0))
                if i >= 2:
                    jj = i - 2
                    pA = psA2[jj % 2] if (not OVL or jj < 4) else psA
                    pB = psB2[jj % 2] if (not OVL or jj < 4) else psBbf
                    step(lambda jj=jj, pA=pA: scalar.activation(
                            xnT.rearrange("p (k t) -> p k t", k=KC)[:, :, jj * 128:(jj + 1) * 128],
                            pA.rearrange("p (k t) -> p k t", k=KC), AF.Identity),
                         (st, pa_t[jj][0]))
                    step(lambda jj=jj, pB=pB: scalar.activation(
                            xqT.rearrange("p (k t) -> p k t", k=KC)[:, :, jj * 128:(jj + 1) * 128],
                            pB.rearrange("p (k t) -> p k t", k=KC), AF.Identity),
                         (st, pa_t[jj][1]))
                if OVL and i >= 7 and (i - 7) < NE3:
                    n0 = i - 7
                    step(lambda n0=n0: scalar.activation(
                            qT[:, n0 * TC:n0 * TC + NH], psQ[n0 % 2][:, 0:NH],
                            AF.Identity, bias=battn[:, n0:n0 + 1], scale=onec),
                         (st, pb_t0[n0]),
                         (sv, P["v_onec"] if n0 == 0 else 0))
            for n0 in range(NE3, EARLY):
                step(lambda n0=n0: scalar.activation(
                        qT[:, n0 * TC:n0 * TC + NH], psQ[n0 % 2][:, 0:NH],
                        AF.Identity, bias=battn[:, n0:n0 + 1], scale=onec),
                     (st, pb_t0[n0]))
            step(lambda: scalar.activation(xsrow[:, :], psB[:, 0:TC], AF.Identity),
                 (st, P["t_xsbc"]))
            last_v_par = {0: 0, 1: 0}
            for n, (kind, c) in enumerate(chunks):
                j = jcol(kind, c)
                if kind == "q":
                    if n < EARLY:
                        step(lambda n=n, c=c, j=j: scalar.activation(
                                qT[:, c * TC + NH:(c + 1) * TC], psQ[n % 2][:, NH:TC],
                                AF.Identity, bias=battn[:, j:j + 1], scale=onec),
                             (st, pb_t[n]))
                    else:
                        step(lambda n=n, c=c, j=j: scalar.activation(
                                qT[:, c * TC:(c + 1) * TC], psQ[n % 2][:, :],
                                AF.Identity, bias=battn[:, j:j + 1], scale=onec),
                             (st, pb_t[n]),
                             (sv, P["v_onec"]))
                    step(lambda c=c: scalar.activation(
                            junk2[:, 0:TC], qT[:, c * TC:(c + 1) * TC], AF.Square,
                            accum_out=Sloc[:, c:c + 1]))
                elif kind == "k":
                    pk = c % 2
                    step(lambda n=n, j=j, pk=pk: scalar.activation(
                            scr1[:, pk * TC:(pk + 1) * TC],
                            psQ[n % 2][:, :], AF.Identity, bias=battn[:, j:j + 1],
                            scale=onec),
                         (st, pb_t[n]),
                         (sv, pb_v[last_v_par[pk]] if last_v_par[pk] else 0),
                         (sv, pa_v[NT - 1]["xqbf"] if last_v_par[pk] == 0 else 0))
                    step(lambda c=c, pk=pk: scalar.activation(
                            junk2[:, 0:TC], scr1[:, pk * TC:(pk + 1) * TC], AF.Square,
                            accum_out=Sloc[:, HQ + c:HQ + c + 1]))
                else:
                    last_v_par[c % 2] = n
                    step(lambda c=c: scalar.activation(
                            junk2[:, 0:TC], junk[:, 0:TC], AF.Identity,
                            accum_out=Sloc[:, 2 * HQ + c:2 * HQ + c + 1]),
                         (sv, pb_v[n]))
            for (what, m) in order:
                if what == "quant":
                    step(lambda m=m: scalar.activation(
                            wqTb[m % 2][:, 0:KH * 128], psA[:, 0:KH * 128],
                            AF.Identity),
                         (st, pf_t[m]["twqA"]),
                         (st, pf_t[m - 2]["mmA"] if m > 1 else 0))
                    if SPLIT:
                        step(lambda m=m: scalar.activation(
                                wqTb[m % 2][:, KH * 128:KC * 128],
                                psA[:, KH * 128:KC * 128], AF.Identity),
                             (st, pf_t[m]["twqB"]),
                             (st, pf_t[m - 2]["mmB"] if m > 1 else 0))
                elif what == "g":
                    step(lambda: scalar.activation(gtmp[:, :], gtmp[:, :], AF.Ln),
                         (sv, P["v_p"]))
                    step(lambda: scalar.activation(gtmp[:, :], gtmp[:, :], AF.Exp,
                                                   scale=-0.5))
                elif what == "e1":
                    step(lambda m=m: scalar.activation(
                            scrU2[m % 4][:, :], scrU[m % 2][:, :], AF.Identity,
                            scale=wscol[:, m:m + 1], bias=biasc[:, m:m + 1]),
                         (sv, pf_v[m]["e1"]),
                         (sv, pf_v[m - 4]["e3"] if m > 3 else 0))
            assert _sa8["c"] == P["a_end"]

        @block.vector
        def _(vector):
            step, _sv8 = stepper(vector, sv)
            vector.wait_ge(sxa, P["sxa"])
            step(lambda: vector.memset(ones_sb[:, :], 1.0))
            step(lambda: vector.memset(onec, 1.0))
            step(lambda: vector.memset(halfc, eps))
            invD = 1.0 / D
            C_MX, C_MN = C_SSQ, C_M2
            for i in range(NT + 1):
                if i < NT:
                    xi = xstage[i % 3]
                    step(lambda i=i, xi=xi: vector.reduce_max(stats[:, 16 + (i & 1):17 + (i & 1)],
                                                              xi[:, :], axis=AX),
                         (sxt[i], 16),
                         (sv, pa_v[i - 2]["dhi"] if i > 1 else 0))
                    step(lambda i=i, xi=xi: vector.tensor_reduce(
                            stats[:, 18 + (i & 1):19 + (i & 1)], xi[:, :],
                            axis=AX, op=ALU.min),
                         (sv, pa_v[i - 2]["dlo"] if i > 1 else 0))
                    step(lambda i=i: vector.tensor_scalar_mul(stc(C_M, i),
                                                              stc(C_SUM, i), invD),
                         (sa, pa_a[i]["sum"]),
                         (sv, pa_v[i - 2]["nmr"] if i > 1 else 0))
                    step(lambda i=i: vector.tensor_mul(stc(C_M2, i), stc(C_M, i),
                                                       stc(C_M, i)))
                    step(lambda i=i: vector.scalar_tensor_tensor(
                            stc(C_A, i), stc(C_SSQ, i), invD, stc(C_M2, i),
                            ALU.mult, ALU.subtract),
                         (sa, pa_a[i]["ssq"]),
                         (sa, pa_a[i - 2]["lnp"] if i > 1 else 0))
                if i >= 1:
                    j = i - 1
                    step(lambda j=j: vector.tensor_scalar(stc(C_NMR, j), stc(C_M, j),
                                                          stc(C_R, j), -1.0,
                                                          ALU.mult, ALU.mult),
                         (sa, pa_a[j]["r"]),
                         (sa, pa_a[j - 2]["xn"] if j > 1 else 0))
                    step(lambda j=j: vector.tensor_sub(
                            stats[:, 16 + (j & 1):17 + (j & 1)],
                            stats[:, 16 + (j & 1):17 + (j & 1)], stc(C_M, j)))
                    step(lambda j=j: vector.tensor_sub(
                            stats[:, 18 + (j & 1):19 + (j & 1)],
                            stc(C_M, j), stats[:, 18 + (j & 1):19 + (j & 1)]))
                    step(lambda j=j: vector.tensor_max(
                            stc(C_XMAX, j),
                            stats[:, 16 + (j & 1):17 + (j & 1)],
                            stats[:, 18 + (j & 1):19 + (j & 1)]))
                    step(lambda j=j: vector.scalar_tensor_tensor(
                            xs_col[j], stc(C_XMAX, j), 1.0 / 127.0, stc(C_R, j),
                            ALU.mult, ALU.mult),
                         (sa, pa_a[j]["r"]))
                    step(lambda j=j: vector.tensor_scalar_max(xs_col[j], xs_col[j],
                                                              1e-12))
                    step(lambda j=j: vector.reciprocal(ixs_col[j], xs_col[j]))
                    step(lambda j=j: vector.tensor_scalar_mul(scr1_i32[:, :],
                                                              scr0[j % 2][:, :],
                                                              ixs_col[j]),
                         (sa, pa_a[j]["xn"]))
                    step(lambda: vector.tensor_copy(xqbf[:, :], scr1_i32[:, :]),
                         (st, pa_t[j - 1][1] if j > 0 else 0))
                    step(lambda j=j: vector.tensor_copy(xnbf[:, :], scr0[j % 2][:, :]),
                         (sa, pa_a[j]["xn"]),
                         (st, pa_t[j - 1][0] if j > 0 else 0))
            assert _sv8["c"] == P["v_pa_end"]

            def emit_quant(m):
                step(lambda m=m: vector.reduce_max(stc(C_XMAX, m), wrow[m % 2][:, :],
                                                   axis=AX, apply_absolute_value=True),
                     ((swrh2 if m < 2 else swr2)[m % 2], P["swr"][m]))
                step(lambda m=m: vector.tensor_scalar_max(stc(C_SUM, m),
                                                          stc(C_XMAX, m), 1e-8))
                step(lambda m=m: vector.tensor_scalar_mul(wscol[:, m:m + 1],
                                                          stc(C_SUM, m), 1.0 / 127.0))
                step(lambda m=m: vector.reciprocal(stc(C_M, m), stc(C_SUM, m)))
                step(lambda m=m: vector.tensor_scalar_mul(winv[:, m:m + 1],
                                                          stc(C_M, m), 127.0))
                step(lambda m=m: vector.tensor_scalar_mul(scr0_i32[:, :],
                                                          wrow[m % 2][:, :],
                                                          winv[:, m:m + 1]))
                step(lambda m=m: vector.tensor_copy(wqbf2[m % 2][:, :], scr0_i32[:, :]),
                     (st, pf_t[m - 2][TWQL] if m > 1 else
                          (pa_t[NT - 1][1] if m == 0 else pa_t[NT - 1][0])))
            prev_v = 0
            vi = 0
            kidx = {}
            for n, (kind, c) in enumerate(chunks):
                if kind == "k":
                    kidx[c] = n
                if kind != "v":
                    continue
                j = jcol(kind, c)
                pk = c % 2
                step(lambda n=n, j=j, pk=pk: vector.scalar_tensor_tensor(
                        junk[:, 0:TC], psQ[n % 2][:, :], battn[:, j:j + 1],
                        scr1[:, pk * TC:(pk + 1) * TC], ALU.add, ALU.mult),
                     (st, pb_t[n]),
                     (sa, pb_a[kidx[c]]["evac"]),
                     (sa, pb_a[prev_v]["accum"] if prev_v else 0))
                prev_v = n
                if vi in early_q:
                    emit_quant(early_q[vi])
                vi += 1
            assert _sv8["c"] == P["v_pb_end"]
            for (what, m) in order:
                if what == "quant":
                    if m not in early_mset:
                        emit_quant(m)
                elif what == "g":
                    step(lambda: vector.tensor_mul(gtmp[:, :], Sglob[:, 0:HQ],
                                                   Sglob[:, HQ:2 * HQ]),
                         (scret, P["scc_ret"]))
                    step(lambda: vector.tensor_mul(g_sb[:, :], Sglob[:, 2 * HQ:3 * HQ],
                                                   gtmp[:, :]),
                         (sa, P["a_rs"]))
                elif what == "e1":
                    step(lambda m=m: vector.tensor_mul(scrU[m % 2][:, :],
                                                       psQ[m % 2][:, :], xsrow[:, :]),
                         (st, pf_t[m][MML]),
                         (sa, P["a_xsrow"] if m == 0 else 0),
                         (sa, pf_a[m - 2]["e2"] if m > 1 else 0))
                elif what == "e3":
                    step(lambda m=m: vector.scalar_tensor_tensor(
                            scrO[m % 2][:, :], qT[:, m * TC:(m + 1) * TC],
                            g_sb[:, m:m + 1],
                            scrU2[m % 4][:, :], ALU.mult, ALU.add),
                         (sa, pf_a[m]["e2"]),
                         (st, pf_t[m - 2]["tout"] if m > 1 else 0))
                else:
                    step(lambda m=m: vector.tensor_tensor(
                            outb[m % 2][:, :], psB[:, 0:TC], xre[m % 2][:, :],
                            ALU.add),
                         (st, pf_t[m]["tout"]),
                         (sxr2[m % 2], 16 * (m // 2 + 1)),
                         (so2[m % 2], P["so"][m - 2] if m > 1 else 0))
            assert _sv8["c"] == P["v_end"]

        @block.tensor
        def _(tensor):
            tcnt = {"c": 0}

            def group(emits, *waits):
                for (s, val) in waits:
                    if val and val > 0:
                        tensor.wait_ge(s, val)
                if tcnt["c"] > 0:
                    tensor.wait_ge(st, tcnt["c"])
                last = None
                for e in emits:
                    last = e()
                last.then_inc(st, 1)
                tcnt["c"] += 1
                return tcnt["c"]

            def lhsT_of(n, k):
                p = pair_of(n)
                sub = (jcol(chunks[n][0], chunks[n][1]) * 128 - pair_col[p]) // 128
                if NSTG and 4 <= n < EARLY:
                    buf = stg[p - 2]
                else:
                    buf = wbuf[p % 2]
                return buf.rearrange("p (k c) -> p k c", k=KC)[:, k, sub * 128:(sub + 1) * 128]

            def emit_h0(n):
                p = pair_of(n)
                if NSTG and 4 <= n < EARLY:
                    waits = [(sstg[p - 2], 16)]
                else:
                    waits = [(sw2[p % 2], P["sw"][n])]
                if n >= 2:
                    waits.append((sa, pb_a0[n - 2]))
                elif n == 0:
                    waits.append((sa, pa_a[3]["evA"]))
                else:
                    waits.append((sa, pa_a[3]["evB"]))
                group([lambda n=n, k=k: tensor.matmul(
                        psQ[n % 2][:, 0:NH], lhsT_of(n, k),
                        xnT[:, k * TC: k * TC + NH],
                        start=(k == 0), stop=(k == KC - 1)) for k in range(KC)],
                      *waits)

            for i in range(NT):
                pA = psA2[i % 2] if (not OVL or i < 4) else psA
                pB = psB2[i % 2] if (not OVL or i < 4) else psBbf
                if not OVL or i < 4:
                    wA = (sa, pa_a[i - 2]["evA"] if i > 1 else 0)
                    wB = (sa, pa_a[i - 2]["evB"] if i > 1 else 0)
                else:
                    wA = (sa, pa_a[i - 1 if i > 4 else 2]["evA"])
                    wB = (sa, pa_a[i - 1 if i > 4 else 2]["evB"])
                group([lambda k=k, pA=pA: tensor.transpose(
                            pA[:, k * 128:(k + 1) * 128],
                            xnbf[:, k * 128:(k + 1) * 128],
                            identb[:, :]) for k in range(KC)],
                      (sxa, P["sxa"]),
                      (sv, pa_v[i]["xnbf"]),
                      wA)
                group([lambda k=k, pB=pB: tensor.transpose(
                            pB[:, k * 128:(k + 1) * 128],
                            xqbf[:, k * 128:(k + 1) * 128],
                            identb[:, :]) for k in range(KC)],
                      (sv, pa_v[i]["xqbf"]),
                      wB)
                if OVL and i >= 5 and (i - 5) < NE3:
                    emit_h0(i - 5)
            for n in range(NE3, EARLY):
                emit_h0(n)
            group([(lambda h=h: tensor.matmul(psB[:, h * NH:(h + 1) * NH], ones_sb[:, :],
                                              xsrow1[:, h * NH:(h + 1) * NH],
                                              start=True, stop=True)) for h in range(HL)],
                  (sxs, 32),
                  (sv, P["v_ones"]),
                  (sa, pa_a[NT - 1]["evB"]))
            for n, (kind, c) in enumerate(chunks):
                waits = []
                if n < EARLY:
                    if n == 0:
                        waits.append((sa, pa_a[NT - 1]["evA"]))
                    if n >= 2:
                        waits.append((sa, pb_a[n - 2]["evac"]))
                    group([lambda n=n, k=k: tensor.matmul(
                            psQ[n % 2][:, NH:TC], lhsT_of(n, k),
                            xnT[:, k * TC + NH: k * TC + TC],
                            start=(k == 0), stop=(k == KC - 1)) for k in range(KC)],
                          *waits)
                    continue
                waits = [(sw2[pair_of(n) % 2], P["sw"][n])]
                if n == 0:
                    waits.append((sa, pa_a[NT - 1]["evB"]))
                if n >= 2:
                    k2, _ = chunks[n - 2]
                    if k2 == "v":
                        waits.append((sv, pb_v[n - 2]))
                    else:
                        waits.append((sa, pb_a[n - 2]["evac"]))
                emits = []
                for h in range(HL):
                    for k in range(KC):
                        emits.append(lambda n=n, k=k, h=h: tensor.matmul(
                            psQ[n % 2][:, h * NH:(h + 1) * NH], lhsT_of(n, k),
                            xnT[:, k * TC + h * NH: k * TC + (h + 1) * NH],
                            start=(k == 0), stop=(k == KC - 1)))
                group(emits, *waits)
            for (what, m) in torder:
                if what == "twqA":
                    waits = [(sv, pf_v[m]["wqbf"]),
                             (sa, pf_a[m - 1]["evWa"] if m > 0 else 0)]
                    if m == 0:
                        waits.append((sa, pa_a[NT - 1]["evA"]))
                    group([lambda k=k, m=m: tensor.transpose(
                            psA[:, k * 128:(k + 1) * 128],
                            wqbf2[m % 2][:, k * 128:(k + 1) * 128],
                            identb[:, :]) for k in range(KH)],
                          *waits)
                elif what == "twqB":
                    group([lambda k=k, m=m: tensor.transpose(
                            psA[:, k * 128:(k + 1) * 128],
                            wqbf2[m % 2][:, k * 128:(k + 1) * 128],
                            identb[:, :]) for k in range(KH, KC)],
                          (sa, pf_a[m - 1]["evWb"] if m > 0 else 0))
                elif what in ("mmA", "mmB"):
                    ka, kb = (0, KH) if what == "mmA" else (KH, KC)
                    if what == "mmA":
                        waits = [(sa, pf_a[m]["evWa"])]
                        if m >= 2:
                            waits.append((sv, pf_v[m - 2]["e1"]))
                        else:
                            waits.append((sv, pb_v[NCH - 2 + m]))
                    else:
                        waits = [(sa, pf_a[m]["evWb"])]
                    emits = []
                    for h in range(HL):
                        for k in range(ka, kb):
                            lhsT = wqTb[m % 2].rearrange("p (k c) -> p k c", k=KC)[:, k, :]
                            emits.append(lambda m=m, k=k, h=h, lhsT=lhsT: tensor.matmul(
                                psQ[m % 2][:, h * NH:(h + 1) * NH], lhsT,
                                xqT[:, k * TC + h * NH: k * TC + (h + 1) * NH],
                                start=(k == 0), stop=(k == KC - 1)))
                    group(emits, *waits)
                else:
                    waits = [(sv, pf_v[m]["e3"])]
                    if m == 0:
                        waits.append((sa, P["a_xsrow"]))
                    else:
                        waits.append((sv, pf_v[m - 1]["e4"]))
                    group([lambda i=i: tensor.transpose(
                            psB[:, i * 128:(i + 1) * 128],
                            scrO[m % 2][:, i * 128:(i + 1) * 128],
                            identf[:, :]) for i in range(NT)],
                          *waits)
            assert tcnt["c"] == P["t_end"]

    meta = dict(D=D, TC=TC, NT=NT, KC=KC, HQ=HQ, OC=OC, FS=FS)
    return nc, meta


def _prep_inputs(x, W, b, wt, bias, NT, KC, HQ, n_cores=N_CORES):
    import ml_dtypes
    D = 128 * KC
    TC = 128 * NT
    OC = 3 * HQ
    FS = HQ
    x2 = np.ascontiguousarray(np.asarray(x, dtype=np.float32).reshape(n_cores * TC, D))
    W = np.ascontiguousarray(np.asarray(W, dtype=np.float32))
    wt = np.ascontiguousarray(np.asarray(wt, dtype=np.float32))
    bcol = np.ascontiguousarray(np.asarray(b, dtype=np.float32).reshape(OC, 128).T)
    biascol = np.ascontiguousarray(np.asarray(bias, dtype=np.float32).reshape(FS, 128).T)
    identb = np.ascontiguousarray(np.eye(128, dtype=ml_dtypes.bfloat16))
    identf = np.ascontiguousarray(np.eye(128, dtype=np.float32))
    in_maps = []
    for cid in range(n_cores):
        in_maps.append({
            "x": x2[cid * TC:(cid + 1) * TC],
            "W": W,
            "wt": wt, "bcol": bcol, "biascol": biascol,
            "identb": identb, "identf": identf,
        })
    return in_maps


def kernel(L0_attn_x, L0_attn_W, L0_attn_b, L0_ffn_weight, L0_ffn_bias):
    NT, KC, HQ = 8, 16, 16
    nc, meta = build_nc(NT, KC, HQ)
    in_maps = _prep_inputs(L0_attn_x, L0_attn_W, L0_attn_b,
                           L0_ffn_weight, L0_ffn_bias, NT, KC, HQ)
    res = run_bass_kernel_spmd(nc, in_maps, core_ids=list(range(N_CORES)))
    out = np.concatenate([np.asarray(r["out"]) for r in res.results], axis=0)
    return out.reshape(1, 8192, 2048).astype(np.float32)
